# version 11
# baseline (speedup 1.0000x reference)
import contextlib
import numpy as np
import concourse.bass as bass
import concourse.mybir as mybir
from concourse.bass_utils import run_bass_kernel_spmd
from concourse.alu_op_type import AluOpType

F32 = mybir.dt.float32
BF16 = mybir.dt.bfloat16
I32 = mybir.dt.int32
AF = mybir.ActivationFunctionType

N = 2048
C = 1024
H = 16
HD = 64
R1, R2 = 4, 8
SCALE = float(32 ** -0.5)
C1 = float(np.float32(72.0 / SCALE))
C2 = float(np.float32(45.0 / SCALE))
TWIN = float(np.float32(C1 + C2))
DFLOOR = -400.0
QS = float(np.float32(1.0 / 127.0))
QS2 = float(np.float32(QS) * np.float32(QS))
HID = 3072
NHC = HID // 128
NW = 4
WQ = 512
NTT = N // 128
EPS = 1e-5


class Sem:
    def __init__(self, nc, es, name):
        self.h = es.enter_context(nc.semaphore(name))
        self.v = 0
        self.name = name


class Buf:
    def __init__(self):
        self.w = {}
        self.r = {}

    def read_waits(self):
        return dict(self.w)

    def write_waits(self):
        m = dict(self.w)
        for s, v in self.r.items():
            m[s] = max(m.get(s, 0), v)
        return m

    def did_read(self, mk):
        if mk:
            s, v = mk
            self.r[s] = max(self.r.get(s, 0), v)

    def did_write(self, mk):
        self.w = {}
        self.r = {}
        if mk:
            s, v = mk
            self.w[s] = v


class Gen:
    ENGS = ("tensor", "scalar", "vector", "gpsimd", "sync")

    def __init__(self, nc, es):
        self.nc = nc
        self.ops = {e: [] for e in self.ENGS}
        self.bufs = {}
        self.eng_sem = {e: Sem(nc, es, "sem_" + e) for e in self.ENGS}

    def B(self, name):
        if name not in self.bufs:
            self.bufs[name] = Buf()
        return self.bufs[name]

    def group(self, eng, fns, reads=(), writes=(), extra_waits=(), dma_sem=None):
        waits = {}
        for nm in reads:
            for s, v in self.B(nm).read_waits().items():
                waits[s] = max(waits.get(s, 0), v)
        for nm in writes:
            for s, v in self.B(nm).write_waits().items():
                waits[s] = max(waits.get(s, 0), v)
        for s, v in extra_waits:
            waits[s] = max(waits.get(s, 0), v)
        for s, v in waits.items():
            assert v <= s.v, f"wait {s.name}>={v} but scheduled only {s.v}"
        sem = dma_sem if dma_sem is not None else self.eng_sem[eng]
        amt = 16 if dma_sem is not None else 1
        mk = None
        n = len(fns)
        for i, fn in enumerate(fns):
            w = list(waits.items()) if i == 0 else []
            inc = None
            if i == n - 1:
                sem.v += amt
                inc = (sem, amt)
                mk = (sem, sem.v)
            self.ops[eng].append((fn, w, inc))
        for nm in reads:
            self.B(nm).did_read(mk)
        for nm in writes:
            self.B(nm).did_write(mk)
        return mk

    def op(self, eng, fn, **kw):
        return self.group(eng, [fn], **kw)

    def wait_only(self, eng, waits):
        self.ops[eng].append((None, list(waits.items()), None))


def build_nc(dbg=False):
    nc = bass.Bass("TRN2", target_bir_lowering=False, debug=False)
    es = contextlib.ExitStack()

    x_d = nc.dram_tensor("x", [N, C], F32, kind="ExternalInput").ap()
    y_d = nc.dram_tensor("y", [N, C], F32, kind="ExternalOutput").ap()
    wq1_d = nc.dram_tensor("wq1t", [C, H * R1], F32, kind="ExternalInput").ap()
    wq2_d = nc.dram_tensor("wq2t", [C, H * R2], F32, kind="ExternalInput").ap()
    wk1_d = nc.dram_tensor("wk1t", [C, H * R1], F32, kind="ExternalInput").ap()
    wk2_d = nc.dram_tensor("wk2t", [C, H * R2], F32, kind="ExternalInput").ap()
    wv_d = nc.dram_tensor("wvt", [C, C], F32, kind="ExternalInput").ap()
    wo_d = nc.dram_tensor("wot", [C, C], F32, kind="ExternalInput").ap()
    bo_d = nc.dram_tensor("bo", [1, C], F32, kind="ExternalInput").ap()
    wg_d = nc.dram_tensor("wgt", [C, HID], I32, kind="ExternalInput").ap()
    wu_d = nc.dram_tensor("wut", [C, HID], I32, kind="ExternalInput").ap()
    wd_d = nc.dram_tensor("wdt", [HID, C], I32, kind="ExternalInput").ap()
    ident_d = nc.dram_tensor("ident", [128, 128], F32, kind="ExternalInput").ap()
    identb_d = nc.dram_tensor("identb", [128, 128], F32, kind="ExternalInput").ap()
    mask2_d = nc.dram_tensor("mask2", [128, 128], F32, kind="ExternalInput").ap()
    maskqm_d = nc.dram_tensor("maskqm", [128, 128], F32, kind="ExternalInput").ap()
    rep1a_d = nc.dram_tensor("rep1a", [80, 8, 128], F32, kind="ExternalInput").ap()
    rep2_d = nc.dram_tensor("rep2", [128, 8, 128], F32, kind="ExternalInput").ap()
    repadp_d = nc.dram_tensor("repadp", [128, 8, 128], F32, kind="ExternalInput").ap()
    bd1_d = nc.dram_tensor("bd1", [128, 16], F32, kind="ExternalInput").ap()
    bd2_d = nc.dram_tensor("bd2", [128, 16], F32, kind="ExternalInput").ap()
    hts_d = nc.dram_tensor("hts", [HID, N], BF16, kind="Internal").ap()
    if dbg:
        t_q1x = nc.dram_tensor("t_q1x", [80, N], F32, kind="ExternalOutput").ap()
        t_q2T = nc.dram_tensor("t_q2T", [128, N], F32, kind="ExternalOutput").ap()
        t_dg = nc.dram_tensor("t_dg", [16, N], F32, kind="ExternalOutput").ap()
        t_nkp = nc.dram_tensor("t_nkp", [16, N], F32, kind="ExternalOutput").ap()
        t_kk = nc.dram_tensor("t_kk", [128, 8 * N], F32, kind="ExternalOutput").ap()
        t_nq = nc.dram_tensor("t_nq", [16, 2 * N], F32, kind="ExternalOutput").ap()
        t_v = nc.dram_tensor("t_v", [128, NTT * H * (HD + 1)], F32, kind="ExternalOutput").ap()
        t_qq = nc.dram_tensor("t_qq", [128, 8 * WQ], F32, kind="ExternalOutput").ap()
        t_E = nc.dram_tensor("t_E", [128, 1024], F32, kind="ExternalOutput").ap()
        t_rr = nc.dram_tensor("t_rr", [64, WQ], F32, kind="ExternalOutput").ap()
        t_ao = nc.dram_tensor("t_ao", [128, 8 * WQ], F32, kind="ExternalOutput").ap()
        t_ln2 = nc.dram_tensor("t_ln2", [128, 8 * WQ], F32, kind="ExternalOutput").ap()
        t_h = nc.dram_tensor("t_h", [128, WQ], F32, kind="ExternalOutput").ap()
        t_x2r = nc.dram_tensor("t_x2r", [128, C], F32, kind="ExternalOutput").ap()
        t_hT = nc.dram_tensor("t_hT", [128, NHC * 128], F32, kind="ExternalOutput").ap()
        t_f0 = nc.dram_tensor("t_f0", [128, WQ], F32, kind="ExternalOutput").ap()
        t_f1 = nc.dram_tensor("t_f1", [128, WQ], F32, kind="ExternalOutput").ap()

    A = lambda name, shape, dt: nc.alloc_sbuf_tensor(name, shape, dt)
    v_sb = A("v_sb", [128, NTT, H, HD + 1], BF16)
    kk_sb = A("kk_sb", [128, 8, N], BF16)
    q1x = A("q1x", [80, N], BF16)
    q2T = A("q2T", [128, N], BF16)
    ident_sb = A("ident_sb", [128, 128], F32)
    identb_sb = A("identb_sb", [128, 128], BF16)
    mask2_sb = A("mask2_sb", [128, 128], BF16)
    maskqm_sb = A("maskqm_sb", [128, 128], BF16)
    Lmq = A("Lmq", [128, 4, 16], F32)
    Dw = A("Dw", [16, 1024], F32)
    Dfull = A("Dfull", [128, 512], BF16)
    rrpad = A("rrpad", [128, 1024], BF16)
    onesel = A("onesel", [128, 64], BF16)
    rep1a_sb = A("rep1a_sb", [80, 8, 128], BF16)
    rep2_sb = A("rep2_sb", [128, 8, 128], BF16)
    repadp_sb = A("repadp_sb", [128, 8, 128], BF16)
    bd1_sb = A("bd1_sb", [128, 16], BF16)
    bd2_sb = A("bd2_sb", [128, 16], BF16)
    ones128 = A("ones128", [1, 128], BF16)
    bo_bf = A("bo_bf", [1, C], BF16)
    bo_b = A("bo_b", [128, C], BF16)
    xp = [A("xp0", [128, C], F32), A("xp1", [128, C], F32)]
    lnp = [A("lnp0", [128, C], BF16), A("lnp1", [128, C], BF16)]
    statsv = [A("stats0", [128, 8], F32), A("stats1", [128, 8], F32)]
    statc = A("statc", [128, 2], F32)
    epsb = statc[:, 0:1]
    tinyb = statc[:, 1:2]
    scrbf = A("scrbf", [128, WQ], BF16)
    sqt = scrbf[:, 384:512]
    sqtb = A("sqtb", [128, 128], BF16)
    rrb = rrpad[0:1, 0:WQ]
    hch = scrbf[:, 0:WQ]
    scr32 = A("scr32", [128, WQ], F32)
    lnr = scr32[0:1, :]
    rrs = scr32[64:128, :]
    ntmp = scr32[0:16, :]
    Eb0 = A("Eb0", [128, 1024], BF16)
    t1 = A("t1", [128, WQ], BF16)
    aoT = A("aoT", [128, 8, WQ], BF16)
    hTt = aoT[:].rearrange("p a b -> p (a b)")[:, 0:NHC * 128].rearrange(
        "p (hc t) -> p hc t", hc=NHC)
    ln2T = A("ln2T", [128, 8, WQ], BF16)
    wgb = [A("wgb%d" % i, [128, 8, 128], BF16) for i in range(3)]
    wub = [A("wub%d" % i, [128, 8, 128], BF16) for i in range(3)]
    wot_sb = A("wot_sb", [128, 8, C], BF16)
    pbuf = A("pbuf", [128, 25600], BF16)

    PB_WV = 0
    PB_K1 = 8192
    PB_K2 = 10240
    PB_NQ1 = 12288
    PB_NQ2 = 14336
    PB_DG = 16384
    PB_NKP = 18432
    PB_LT0 = 20480
    PB_LT1 = 21504
    PB_WQ1 = 22528
    PB_WQ2 = 23040
    PB_WK1 = 24064
    PB_WK2 = 24576

    def wvT(cc, f0, f1):
        return pbuf[:, PB_WV + cc * C + f0: PB_WV + cc * C + f1]

    def k1T(t0, t1):
        return pbuf[0:64, PB_K1 + t0: PB_K1 + t1]

    def k2T(t0, t1):
        return pbuf[0:128, PB_K2 + t0: PB_K2 + t1]

    def nq(i, t0, t1):
        off = PB_NQ1 if i == 0 else PB_NQ2
        return pbuf[0:16, off + t0: off + t1]

    def dg(t0, t1):
        return pbuf[0:16, PB_DG + t0: PB_DG + t1]

    def nkp(t0, t1):
        return pbuf[0:16, PB_NKP + t0: PB_NKP + t1]

    def ping2(t0, t1):
        return pbuf[0:16, PB_NQ2 + t0: PB_NQ2 + t1]

    def wq1w(cc):
        return pbuf[:, PB_WQ1 + cc * 64: PB_WQ1 + (cc + 1) * 64]

    def wq2w(cc):
        return pbuf[:, PB_WQ2 + cc * 128: PB_WQ2 + (cc + 1) * 128]

    def wk1w(cc):
        return pbuf[:, PB_WK1 + cc * 64: PB_WK1 + (cc + 1) * 64]

    def wk2w(cc):
        return pbuf[:, PB_WK2 + cc * 128: PB_WK2 + (cc + 1) * 128]

    def lntp(p):
        off = PB_LT0 if p == 0 else PB_LT1
        return pbuf[:, off: off + 1024].rearrange("p (cc t) -> p cc t", cc=8)

    def wdT(hc, f0, f1):
        return pbuf[:, hc * C + f0: hc * C + f1]

    Eb1 = pbuf[:, PB_WK2:PB_WK2 + 1024]

    def qqh(hh):
        return pbuf[:, PB_WV + hh * 512: PB_WV + (hh + 1) * 512]

    SA = es.enter_context(nc.psum_tensor("SA", [128, 1024], F32))
    SA2 = es.enter_context(nc.psum_tensor("SA2", [128, 1024], F32))
    accb = [es.enter_context(nc.psum_tensor("acc0", [128, 512], F32)),
            es.enter_context(nc.psum_tensor("acc1", [128, 512], F32))]
    gps = es.enter_context(nc.psum_tensor("gps", [128, 512], F32))
    ups = es.enter_context(nc.psum_tensor("ups", [128, 512], F32))

    G = Gen(nc, es)
    sGw = Sem(nc, es, "sGw")
    sSx = [Sem(nc, es, "sSx0"), Sem(nc, es, "sSx1")]
    sSy = Sem(nc, es, "sSy")
    sTp = [Sem(nc, es, "sTp0"), Sem(nc, es, "sTp1")]
    sT2 = Sem(nc, es, "sT2")
    sWg = Sem(nc, es, "sWg")
    sWu = Sem(nc, es, "sWu")
    sHw = [Sem(nc, es, "sHw%d" % i) for i in range(NW)]
    sHl = Sem(nc, es, "sHl")
    sTap = Sem(nc, es, "sTap")

    from concourse.hw_specs import get_activation_tables
    tabs = list(get_activation_tables(nc.m.arch).items())

    def set_with(*fs):
        for i, (nm, funcs) in enumerate(tabs):
            if all(f in funcs for f in fs):
                return i
        raise ValueError(f"no activation table set with {fs}")
    NLX_ID = set_with(AF.Exp, AF.Ln)
    SILU_ID = set_with(AF.Silu)

    def load_table(tid):
        def f(e):
            return e.add_instruction(mybir.InstLoadActFuncSet(
                name=nc.get_next_instruction_name(), act_func_set_id=tid,
                ins=[], outs=[]))
        return f

    op, group = G.op, G.group

    def tap(buf_names, out_ap, in_ap):
        if not dbg:
            return
        def f(e):
            with nc.allow_non_contiguous_dma(reason="debug tap"):
                return e.dma_start(out=out_ap, in_=in_ap)
        op("gpsimd", f, reads=list(buf_names), dma_sem=sTap)

    G.ops["scalar"].append((load_table(NLX_ID), [], None))
    op("gpsimd", lambda e: e.memset(v_sb[:, :, :, HD:HD + 1], 1.0), writes=["vones"])
    op("gpsimd", lambda e: e.memset(ones128[:], 1.0), writes=["ones128"])
    op("vector", lambda e: e.memset(epsb, EPS), writes=["statc"])
    op("vector", lambda e: e.memset(tinyb, 1e-30), writes=["statc"])
    op("gpsimd", lambda e: e.memset(kk_sb[:], 0.0), writes=["kk"])
    op("gpsimd", lambda e: e.memset(q1x[64:80, :], 0.0), writes=["q1x"])
    op("gpsimd", lambda e: e.memset(Dfull[:], 0.0), writes=["Dfull"])
    op("gpsimd", lambda e: e.memset(rrpad[:], 0.0), writes=["scrbf_rr"])
    op("gpsimd", lambda e: e.memset(onesel[:], 0.0), writes=["onesel"])
    op("gpsimd", lambda e: e.memset(onesel[0:1, :], 1.0), writes=["onesel"])
    op("gpsimd", lambda e: e.memset(pbuf[64:80, PB_K1:PB_K1 + N], -1.0), writes=["k1Tb"])
    for nm, dst, srcd in [("ident", ident_sb[:], ident_d), ("identb", identb_sb[:], identb_d),
                          ("mask2", mask2_sb[:], mask2_d), ("maskqm", maskqm_sb[:], maskqm_d),
                          ("rep1a", rep1a_sb[:], rep1a_d), ("rep2", rep2_sb[:], rep2_d),
                          ("repadp", repadp_sb[:], repadp_d),
                          ("bd1", bd1_sb[:], bd1_d), ("bd2", bd2_sb[:], bd2_d),
                          ("bo_bf", bo_bf[:], bo_d)]:
        op("gpsimd", (lambda d, s: lambda e: e.dma_start(out=d, in_=s))(dst, srcd),
           writes=[nm], dma_sem=sGw)
    op("gpsimd", lambda e: e.dma_start(
        out=pbuf[:, PB_WV:PB_WV + 8 * C].rearrange("p (cc f) -> p cc f", cc=8),
        in_=wv_d.rearrange("(cc p) f -> p cc f", p=128)), writes=["wvT"], dma_sem=sGw)
    op("gpsimd", lambda e: e.dma_start(
        out=pbuf[:, PB_WQ1:PB_WQ1 + 512].rearrange("p (cc f) -> p cc f", cc=8),
        in_=wq1_d.rearrange("(cc p) f -> p cc f", p=128)), writes=["wq1w"], dma_sem=sGw)
    op("gpsimd", lambda e: e.dma_start(
        out=pbuf[:, PB_WQ2:PB_WQ2 + 1024].rearrange("p (cc f) -> p cc f", cc=8),
        in_=wq2_d.rearrange("(cc p) f -> p cc f", p=128)), writes=["wq2w"], dma_sem=sGw)
    op("gpsimd", lambda e: e.dma_start(
        out=pbuf[:, PB_WK1:PB_WK1 + 512].rearrange("p (cc f) -> p cc f", cc=8),
        in_=wk1_d.rearrange("(cc p) f -> p cc f", p=128)), writes=["wk1w"], dma_sem=sGw)
    op("gpsimd", lambda e: e.dma_start(
        out=pbuf[:, PB_WK2:PB_WK2 + 1024].rearrange("p (cc f) -> p cc f", cc=8),
        in_=wk2_d.rearrange("(cc p) f -> p cc f", p=128)), writes=["wk2w"], dma_sem=sGw)
    op("gpsimd", lambda e: e.dma_start(out=wot_sb[:], in_=wo_d.rearrange("(cc p) f -> p cc f", p=128)),
       writes=["wot"], dma_sem=sGw)
    for _nm in ("ident", "identb", "mask2", "maskqm", "rep1a", "rep2", "repadp",
                "bd1", "bd2", "bo_bf", "wvT", "wq1w", "wq2w", "wk1w", "wk2w", "wot"):
        G.B(_nm).did_write((sGw, sGw.v))

    def ln_full(p, xname, x_ap, outname, out_ap):
        st = statsv[p]
        sn = "stats%d" % p
        s1, s2, s2b = st[:, 0:1], st[:, 1:2], st[:, 2:3]
        mst, vst, rstd, nmr = st[:, 3:4], st[:, 4:5], st[:, 5:6], st[:, 6:7]
        sq = Eb0[:, p * 512:(p + 1) * 512]
        op("vector", (lambda a=s1: lambda e: e.reduce_sum(a, x_ap, axis=mybir.AxisListType.X))(),
           reads=[xname], writes=[sn])
        op("scalar", (lambda a=s2, q=sq: lambda e: e.activation(
            q, x_ap[:, 0:512], AF.Square, accum_out=a))(),
           reads=[xname], writes=[sn, "E0"])
        op("scalar", (lambda a=s2b, q=sq: lambda e: e.activation(
            q, x_ap[:, 512:1024], AF.Square, accum_out=a))(),
           reads=[xname], writes=[sn, "E0"])
        op("vector", (lambda a=s2, b=s2b: lambda e: e.tensor_tensor(a, a, b, AluOpType.add))(),
           reads=[sn], writes=[sn])
        op("vector", (lambda a=mst, b=s1: lambda e: e.tensor_scalar_mul(a, b, 1.0 / C))(),
           reads=[sn], writes=[sn])
        op("vector", (lambda a=nmr, b=mst: lambda e: e.tensor_tensor(a, b, b, AluOpType.mult))(),
           reads=[sn], writes=[sn])
        op("vector", (lambda a=vst, b=s2: lambda e: e.tensor_scalar_mul(a, b, 1.0 / C))(),
           reads=[sn], writes=[sn])
        op("vector", (lambda a=vst, b=nmr: lambda e: e.tensor_tensor(a, a, b, AluOpType.subtract))(),
           reads=[sn], writes=[sn])
        op("scalar", (lambda a=vst: lambda e: e.activation(a, a, AF.Ln, bias=epsb))(),
           reads=[sn, "statc"], writes=[sn])
        op("scalar", (lambda a=rstd, b=vst: lambda e: e.activation(a, b, AF.Exp, scale=-0.5))(),
           reads=[sn], writes=[sn])
        op("vector", (lambda a=nmr, b=mst, c=rstd: lambda e: e.tensor_scalar(
            a, b, c, -1.0, AluOpType.mult, AluOpType.mult))(),
           reads=[sn], writes=[sn])
        op("vector", (lambda c=rstd, d=nmr: lambda e: e.tensor_scalar(
            out_ap, x_ap, c, d, AluOpType.mult, AluOpType.add))(),
           reads=[xname, sn], writes=[outname])

    for tt in range(NTT):
        pi = tt % 2
        xn, ln_, ltn = "xp%d" % pi, "lnp%d" % pi, "lnt%d" % pi
        t0, t128 = tt * 128, (tt + 1) * 128
        op("gpsimd", (lambda t, p: lambda e: e.dma_start(out=xp[p][:], in_=x_d[t * 128:(t + 1) * 128, :]))(tt, pi),
           writes=[xn], dma_sem=sSx[pi])
        ln_full(pi, xn, xp[pi][:], ln_, lnp[pi][:])
        op("sync", (lambda p: lambda e: e.dma_start_transpose(lntp(p), lnp[p][:]))(pi),
           reads=[ln_], writes=[ltn], dma_sem=sTp[pi])
        fns = []
        for (wfn, M, col, wnm) in ((wq1w, 64, 0, "wq1w"), (wq2w, 128, 128, "wq2w"),
                                   (wk1w, 64, 256, "wk1w"), (wk2w, 128, 384, "wk2w")):
            for cc in range(8):
                def f(w=wfn, m=M, c=cc, co=col, p=pi):
                    def g(e):
                        return e.matmul(gps[0:m, co:co + 128], w(c), lntp(p)[:, c, :],
                                        start=(c == 0), stop=(c == 7), skip_group_check=True)
                    return g
                fns.append(f())
        group("tensor", fns, reads=[ltn, "wq1w", "wq2w", "wk1w", "wk2w"], writes=["gps"])
        fns = []
        for half in range(2):
            for cc in range(8):
                def f(hf=half, c=cc, p=pi):
                    def g(e):
                        return e.matmul(SA[:, hf * 512:(hf + 1) * 512],
                                        lntp(p)[:, c, :], wvT(c, hf * 512, (hf + 1) * 512),
                                        start=(c == 0), stop=(c == 7), skip_group_check=True)
                    return g
                fns.append(f())
        group("tensor", fns, reads=[ltn, "wvT"], writes=["SA"])
        op("vector", (lambda a, b: lambda e: e.tensor_copy(q1x[0:64, a:b], gps[0:64, 0:128]))(t0, t128),
           reads=["gps"], writes=["q1x"])
        op("vector", (lambda a, b: lambda e: e.tensor_copy(q2T[:, a:b], gps[:, 128:256]))(t0, t128),
           reads=["gps"], writes=["q2T"])
        op("vector", (lambda a, b: lambda e: e.tensor_copy(k1T(a, b), gps[0:64, 256:384]))(t0, t128),
           reads=["gps"], writes=["k1Tb"])
        mk = op("vector", (lambda a, b: lambda e: e.tensor_copy(k2T(a, b), gps[:, 384:512]))(t0, t128),
                reads=["gps"], writes=["k2Tb"])
        G.B("gps").did_read(mk)
        op("vector", (lambda t: lambda e: e.tensor_copy(
            v_sb[:, t, :, 0:HD], SA[:, 0:1024].rearrange("p (h d) -> p h d", h=H)))(tt),
           reads=["SA"], writes=["vbuf"])
        sq_ap, sq_nm = (sqt, "scrbf") if pi == 0 else (sqtb[:], "sqtb")
        for (srcfn, srcnm, sp, bdnm, bdap, kind) in (
                (lambda a=t0, b=t128: q1x[0:64, a:b], "q1x", 64, "bd1", bd1_sb[:], "q1"),
                (lambda a=t0, b=t128: q2T[:, a:b], "q2T", 128, "bd2", bd2_sb[:], "q2"),
                (lambda a=t0, b=t128: k1T(a, b), "k1Tb", 64, "bd1", bd1_sb[:], "k1"),
                (lambda a=t0, b=t128: k2T(a, b), "k2Tb", 128, "bd2", bd2_sb[:], "k2")):
            op("scalar", (lambda s_, p_, q=sq_ap: lambda e: e.activation(q[0:p_, :], s_(), AF.Square))(srcfn, sp),
               reads=[srcnm], writes=[sq_nm])
            op("tensor", (lambda b_, q=sq_ap: lambda e: e.matmul(
                ups[0:16, 0:128], b_, q[:, :], start=True, stop=True,
                skip_group_check=True))(bdap),
               reads=[sq_nm, bdnm], writes=["ups"])
            if kind in ("q1", "q2"):
                i = 0 if kind == "q1" else 1
                op("vector", (lambda ii, a, b: lambda e: e.tensor_copy(nq(ii, a, b), ups[0:16, 0:128]))(i, t0, t128),
                   reads=["ups"], writes=["nq%d" % i])
            elif kind == "k1":
                op("vector", (lambda a, b: lambda e: e.tensor_copy(nkp(a, b), ups[0:16, 0:128]))(t0, t128),
                   reads=["ups"], writes=["nkpb"])
            else:
                op("vector", (lambda a, b: lambda e: e.tensor_tensor(
                    nkp(a, b), nkp(a, b), ups[0:16, 0:128], AluOpType.mult))(t0, t128),
                   reads=["ups", "nkpb"], writes=["nkpb"])
        op("vector", (lambda a, b, q=sq_ap: lambda e: e.tensor_tensor(
            q[0:64, :], q1x[0:64, a:b], k1T(a, b), AluOpType.mult))(t0, t128),
           reads=["q1x", "k1Tb"], writes=[sq_nm])
        op("tensor", (lambda q=sq_ap: lambda e: e.matmul(ups[0:16, 0:128], bd1_sb[:], q[:, :],
                                        start=True, stop=True, skip_group_check=True))(),
           reads=[sq_nm, "bd1"], writes=["ups"])
        op("vector", (lambda a, b: lambda e: e.tensor_copy(dg(a, b), ups[0:16, 0:128]))(t0, t128),
           reads=["ups"], writes=["dgb"])
        op("vector", (lambda a, b, q=sq_ap: lambda e: e.tensor_tensor(
            q[:, :], q2T[:, a:b], k2T(a, b), AluOpType.mult))(t0, t128),
           reads=["q2T", "k2Tb"], writes=[sq_nm])
        op("tensor", (lambda q=sq_ap: lambda e: e.matmul(ups[0:16, 0:128], bd2_sb[:], q[:, :],
                                        start=True, stop=True, skip_group_check=True))(),
           reads=[sq_nm, "bd2"], writes=["ups"])
        op("vector", (lambda a, b: lambda e: e.tensor_tensor(
            dg(a, b), dg(a, b), ups[0:16, 0:128], AluOpType.mult))(t0, t128),
           reads=["ups", "dgb"], writes=["dgb"])

    for g8 in range(8):
        for qc in range(4):
            a, b = qc * 512, (qc + 1) * 512
            op("tensor", (lambda gg, aa, bb: lambda e: e.matmul(
                gps[:, 0:512], rep1a_sb[:, gg, :], pbuf[0:80, PB_K1 + aa:PB_K1 + bb],
                start=True, stop=True, skip_group_check=True))(g8, a, b),
               reads=["rep1a", "k1Tb"], writes=["gps"])
            op("tensor", (lambda gg, aa, bb: lambda e: e.matmul(
                ups[:, 0:512], rep2_sb[:, gg, :], k2T(aa, bb),
                start=True, stop=True, skip_group_check=True))(g8, a, b),
               reads=["rep2", "k2Tb"], writes=["ups"])
            op("scalar", lambda e: e.activation(t1[:], gps[:, 0:512], AF.Copy),
               reads=["gps"], writes=["t1"])
            op("vector", (lambda gg, aa, bb: lambda e: e.tensor_tensor(
                kk_sb[0:32, gg, aa:bb], t1[0:32, :], ups[0:32, 0:512], AluOpType.mult))(g8, a, b),
               reads=["t1", "ups"], writes=["kk"])
            op("scalar", (lambda gg, aa, bb: lambda e: e.activation(
                kk_sb[32:33, gg, aa:bb], t1[32:33, :], AF.Copy))(g8, a, b),
               reads=["t1"], writes=["kk"])
            op("vector", (lambda gg, aa, bb: lambda e: e.tensor_tensor(
                kk_sb[64:96, gg, aa:bb], t1[64:96, :], ups[64:96, 0:512], AluOpType.mult))(g8, a, b),
               reads=["t1", "ups"], writes=["kk"])
            op("scalar", (lambda gg, aa, bb: lambda e: e.activation(
                kk_sb[96:97, gg, aa:bb], t1[96:97, :], AF.Copy))(g8, a, b),
               reads=["t1"], writes=["kk"])

    op("vector", lambda e: e.tensor_tensor(nq(0, 0, N), nq(0, 0, N), nq(1, 0, N),
                                           AluOpType.mult),
       reads=["nq0", "nq1"], writes=["nq0"])
    op("vector", lambda e: e.memset(nkp(0, 128), 0.0), reads=["nkpb"], writes=["nkpb"])
    scan_src, scan_dst = nkp, ping2
    sbufs = ["nkpb", "nq1"]
    step = 1
    while step < N:
        sfn, dfn = (nkp, ping2) if sbufs[0] == "nkpb" else (ping2, nkp)
        op("vector", (lambda s_, d_, st: lambda e: e.tensor_tensor(
            d_(st, N), s_(st, N), s_(0, N - st), AluOpType.max))(sfn, dfn, step),
           reads=[sbufs[0]], writes=[sbufs[1]])
        op("vector", (lambda s_, d_, st: lambda e: e.tensor_copy(d_(0, st), s_(0, st)))(sfn, dfn, step),
           reads=[sbufs[0]], writes=[sbufs[1]])
        sbufs = [sbufs[1], sbufs[0]]
        step *= 2
    NKP_FINAL = sbufs[0]
    nkp_fin = nkp if NKP_FINAL == "nkpb" else ping2


    group("tensor", [
        (lambda hf: lambda e: e.matmul(SA[:, hf * 512:(hf + 1) * 512], ones128[:],
                                       bo_bf[:, hf * 512:(hf + 1) * 512],
                                       start=True, stop=True, skip_group_check=True))(hf)
        for hf in range(2)],
        reads=["ones128", "bo_bf"], writes=["SA"])
    op("vector", lambda e: e.tensor_copy(bo_b[:], SA[:, 0:1024]), reads=["SA"], writes=["bo_b"])

    if dbg:
        tap(["q1x"], t_q1x, q1x[0:80, :])
        tap(["q2T"], t_q2T, q2T[:])
        tap(["dgb"], t_dg, dg(0, N))
        tap([NKP_FINAL], t_nkp, nkp_fin(0, N))
        tap(["kk"], t_kk, kk_sb[:].rearrange("p g n -> p (g n)"))
        tap(["nq0"], t_nq[0:16, 0:N], nq(0, 0, N))
        tap(["nq1"], t_nq[0:16, N:2 * N], nq(1, 0, N))
        tap(["vbuf"], t_v, v_sb[:].rearrange("p a b c -> p (a b c)"))

    op("gpsimd", lambda e: e.memset(pbuf[:, PB_WV:PB_WV + 16 * 512], 0.0),
       writes=["qqh"], extra_waits=list(G.B("wvT").write_waits().items()))
    for _s, _v in G.B("wk2w").write_waits().items():
        G.B("E1").r[_s] = max(G.B("E1").r.get(_s, 0), _v)

    Dw0 = Dw[0:16, 0:512]
    Dw1 = Dw[0:16, 512:1024]
    SAb, SAn = [SA, SA2], ["SA", "SA2"]
    Ebb, Ebn = [Eb0, Eb1], ["E0", "E1"]
    ktpar = [0]
    hw_marks = {}
    for qb in range(NW):
        q0 = qb * WQ
        nkt = 4 * qb + 4
        def dpre_units(qb):
            q0 = qb * WQ

            def emit_build(g8):
                op("tensor", (lambda gg, q: lambda e: e.matmul(
                    SA[:, 0:512], rep1a_sb[:, gg, :], q1x[0:80, q:q + WQ],
                    start=True, stop=True, skip_group_check=True))(g8, q0),
                   reads=["rep1a", "q1x"], writes=["SA"])
                op("tensor", (lambda gg, q: lambda e: e.matmul(
                    SA[:, 512:1024], rep2_sb[:, gg, :], q2T[:, q:q + WQ],
                    start=True, stop=True, skip_group_check=True))(g8, q0),
                   reads=["rep2", "q2T"], writes=["SA"])
                op("scalar", lambda e: e.activation(t1[:], SA[:, 0:512], AF.Copy),
                   reads=["SA"], writes=["t1"])
                op("vector", (lambda gg: lambda e: e.tensor_tensor(
                    qqh(2 * gg)[0:32, :], t1[0:32, :], SA[0:32, 512:1024], AluOpType.mult))(g8),
                   reads=["t1", "SA"], writes=["qqh"])
                op("scalar", (lambda gg: lambda e: e.activation(
                    qqh(2 * gg)[32:33, :], t1[32:33, :], AF.Copy))(g8),
                   reads=["t1"], writes=["qqh"])
                op("vector", (lambda gg: lambda e: e.tensor_tensor(
                    qqh(2 * gg + 1)[64:96, :], t1[64:96, :], SA[64:96, 512:1024], AluOpType.mult))(g8),
                   reads=["t1", "SA"], writes=["qqh"])
                op("scalar", (lambda gg: lambda e: e.activation(
                    qqh(2 * gg + 1)[96:97, :], t1[96:97, :], AF.Copy))(g8),
                   reads=["t1"], writes=["qqh"])

            def emit_minis(g8):
                for j2 in range(2):
                    hh = 2 * g8 + j2
                    co = 512 * (hh % 2)
                    fns = []
                    for qt in range(4):
                        masked = (qb == 0 and qt == 0)
                        fns.append((lambda hx=hh, t=qt, c=co, m=masked: lambda e: e.matmul(
                            SA2[:, c + t * 128:c + (t + 1) * 128],
                            qqh(hx)[:, t * 128:(t + 1) * 128], kk_sb[:, hx // 2, 0:128],
                            start=True, stop=not m, skip_group_check=True))())
                        if masked:
                            fns.append((lambda c=co: lambda e: e.matmul(
                                SA2[:, c:c + 128], identb_sb[:], maskqm_sb[:],
                                start=False, stop=True, skip_group_check=True))())
                    group("tensor", fns, reads=["qqh", "kk", "identb", "maskqm"], writes=["SA2"])
                    for qt in range(4):
                        op("vector", (lambda hx=hh, t=qt, c=co: lambda e: e.reduce_max(
                            Lmq[:, t, hx:hx + 1], SA2[:, c + t * 128:c + (t + 1) * 128],
                            axis=mybir.AxisListType.X))(),
                           reads=["SA2"], writes=["Lmq"])

            units = []
            for g8 in range(8):
                units.append((lambda gg=g8: emit_build(gg)))
                if g8 > 0:
                    units.append((lambda gg=g8 - 1: emit_minis(gg)))
            units.append(lambda: emit_minis(7))

            def emit_dchain():
                for qt in range(4):
                    op("tensor", (lambda t: lambda e: e.transpose(
                        SA[0:16, t * 128:(t + 1) * 128], Lmq[:, t, :], ident_sb[:]))(qt),
                       reads=["Lmq", "ident"], writes=["SA"])
                op("vector", lambda e: e.tensor_copy(Dw0, SA[0:16, 0:512]),
                   reads=["SA"], writes=["Dwb"])
                op("vector", (lambda q: lambda e: e.tensor_tensor(
                    ntmp, nq(0, q, q + WQ), nkp_fin(q, q + WQ), AluOpType.mult))(q0),
                   reads=["nq0", NKP_FINAL], writes=["scr32"])
                op("scalar", lambda e: e.activation(ntmp, ntmp, AF.Ln, bias=tinyb[0:16]),
                   reads=["scr32", "statc"], writes=["scr32"])
                op("scalar", lambda e: e.activation(ntmp, ntmp, AF.Exp, scale=0.5),
                   reads=["scr32"], writes=["scr32"])
                op("vector", lambda e: e.tensor_tensor(Dw1, ntmp, Dw0, AluOpType.max),
                   reads=["scr32", "Dwb"], writes=["Dwb"])
                op("vector", (lambda q: lambda e: e.tensor_tensor(
                    Dw0, Dw0, dg(q, q + WQ), AluOpType.max))(q0),
                   reads=["Dwb", "dgb"], writes=["Dwb"])
                op("vector", lambda e: e.tensor_scalar(Dw0, Dw0, float(C2), None, AluOpType.add),
                   reads=["Dwb"], writes=["Dwb"])
                op("vector", lambda e: e.tensor_tensor(Dw0, Dw0, Dw1, AluOpType.min),
                   reads=["Dwb"], writes=["Dwb"])
                op("vector", lambda e: e.tensor_scalar(Dw1, Dw1, float(C1), None, AluOpType.subtract),
                   reads=["Dwb"], writes=["Dwb"])
                op("vector", lambda e: e.tensor_tensor(Dw0, Dw0, Dw1, AluOpType.max),
                   reads=["Dwb"], writes=["Dwb"])
                op("vector", lambda e: e.tensor_scalar(Dw0, Dw0, float(DFLOOR), None, AluOpType.max),
                   reads=["Dwb"], writes=["Dwb"])
                op("vector", lambda e: e.tensor_copy(Dfull[0:16, :], Dw0),
                   reads=["Dwb"], writes=["Dfull"])
            units.append(emit_dchain)

            def emit_augfix(g8):
                op("tensor", (lambda gg: lambda e: e.matmul(
                    SA[:, 0:512], repadp_sb[:, gg, :], Dfull[:, :],
                    start=True, stop=True, skip_group_check=True))(g8),
                   reads=["repadp", "Dfull"], writes=["SA"])
                op("scalar", (lambda gg: lambda e: e.activation(
                    qqh(2 * gg)[32:33, :], SA[32:33, 0:512], AF.Copy))(g8),
                   reads=["SA"], writes=["qqh"])
                op("scalar", (lambda gg: lambda e: e.activation(
                    qqh(2 * gg + 1)[96:97, :], SA[96:97, 0:512], AF.Copy))(g8),
                   reads=["SA"], writes=["qqh"])
            for g8 in range(8):
                units.append((lambda gg=g8: emit_augfix(gg)))
            return units

        if qb == 0:
            for u in dpre_units(0):
                u()
        if dbg and qb == 0:
            tap(["qqh"], t_qq, pbuf[:, PB_WV:PB_WV + 8 * WQ])
        deferred = [None]

        def flush_deferred():
            if deferred[0] is not None:
                for f in deferred[0]:
                    f()
                deferred[0] = None

        for h in range(H):
            a = accb[h % 2]
            an = "acc%d" % (h % 2)
            npairs = (nkt + 1) // 2

            def emit_S(ktp, h=h, a=a, an=an):
                kts = [kt for kt in (2 * ktp, 2 * ktp + 1) if kt < nkt]
                pp = (ktpar[0] + ktp) % 2
                SAc, SAnm = SAb[pp], SAn[pp]
                fns = []
                rd = ["kk", "qqh"]
                for i, kt in enumerate(kts):
                    d = kt - 4 * qb
                    qlo = 128 * max(0, d)
                    diag = d >= 0
                    fns.append((lambda hx=h, k=kt, ql=qlo, ii=i, p=SAc, dgm=diag:
                                lambda e: e.matmul(
                                    p[:, ii * 512 + ql: ii * 512 + 512],
                                    kk_sb[:, hx // 2, k * 128:(k + 1) * 128],
                                    qqh(hx)[:, ql:WQ],
                                    start=True, stop=not dgm, skip_group_check=True))())
                    if diag:
                        fns.append((lambda ql=qlo, ii=i, p=SAc: lambda e: e.matmul(
                            p[:, ii * 512 + ql: ii * 512 + ql + 128],
                            identb_sb[:], mask2_sb[:],
                            start=False, stop=True, skip_group_check=True))())
                        rd = ["kk", "qqh", "identb", "mask2"]
                group("tensor", fns, reads=rd, writes=[SAnm])

            def emit_exp(ktp, h=h):
                kts = [kt for kt in (2 * ktp, 2 * ktp + 1) if kt < nkt]
                pp = (ktpar[0] + ktp) % 2
                SAc, SAnm = SAb[pp], SAn[pp]
                Ebc, Ebnm = Ebb[pp], Ebn[pp]
                wid = 512 * len(kts)
                op("scalar", (lambda w, p, eb: lambda e: e.activation(
                    eb[:, 0:w], p[:, 0:w], AF.Exp, scale=SCALE))(wid, SAc, Ebc),
                   reads=[SAnm], writes=[Ebnm])
                if dbg and qb == 0 and h == 0 and ktp == 0:
                    tap([Ebnm], t_E, Ebc[:])

            def emit_V(ktp, h=h, a=a, an=an):
                kts = [kt for kt in (2 * ktp, 2 * ktp + 1) if kt < nkt]
                pp = (ktpar[0] + ktp) % 2
                Ebc, Ebnm = Ebb[pp], Ebn[pp]
                for i, kt in enumerate(kts):
                    qlo = 128 * max(0, kt - 4 * qb)
                    mk = op("tensor", (lambda k, ii, ql, hx, aa, nk, eb: lambda e: e.matmul(
                        aa[0:HD + 1, ql:WQ], v_sb[:, k, hx, :],
                        eb[:, ii * 512 + ql: ii * 512 + 512],
                        start=(k == 0), stop=(k == nk - 1),
                        skip_group_check=True))(kt, i, qlo, h, a, nkt, Ebc),
                        reads=[Ebnm, "vbuf", "vones"],
                        writes=([an] if kt == 0 else []))
                    if kt == nkt - 1:
                        G.B(an).did_write(mk)

            emit_S(0)
            for t in range(npairs):
                if t + 1 < npairs:
                    emit_S(t + 1)
                if t == 0:
                    flush_deferred()
                emit_exp(t)
                emit_V(t)
            ktpar[0] += npairs
            if h % 2 == 0:
                op("vector", lambda e: e.tensor_copy(Dw[0:1, 0:512], accb[0][HD:HD + 1, :]),
                   reads=["acc0"], writes=["Dwb"])
            else:
                op("vector", lambda e: e.tensor_copy(Dw[0:1, 512:1024], accb[1][HD:HD + 1, :]),
                   reads=["acc1"], writes=["Dwb"])
                op("scalar", lambda e: e.activation(Dw[0:1, 0:1024], Dw[0:1, 0:1024],
                                                    AF.Ln, bias=tinyb[0:1]),
                   reads=["Dwb", "statc"], writes=["Dwb"])
                op("scalar", lambda e: e.activation(rrpad[0:1, 0:1024], Dw[0:1, 0:1024],
                                                    AF.Exp, scale=-1.0),
                   reads=["Dwb"], writes=["scrbf_rr"])

                def fin_pair(h=h):
                    for hx in (h - 1, h):
                        co = 512 * (hx % 2)
                        aa, an2 = accb[hx % 2], "acc%d" % (hx % 2)
                        op("tensor", (lambda c=co: lambda e: e.matmul(
                            gps[0:64, 0:512], onesel[:], rrpad[:, c:c + 512],
                            start=True, stop=True, skip_group_check=True))(),
                           reads=["onesel", "scrbf_rr"], writes=["gps"])
                        op("vector", lambda e: e.tensor_copy(rrs, gps[0:64, 0:512]),
                           reads=["gps"], writes=["scr32"])
                        if dbg and qb == 0 and hx == 0:
                            tap(["scr32"], t_rr, rrs)
                        op("vector", (lambda hy=hx, ab=aa: lambda e: e.tensor_tensor(
                            aoT[64 * (hy % 2):64 * (hy % 2) + 64, hy // 2, :],
                            ab[0:HD, :], rrs, AluOpType.mult))(),
                           reads=[an2, "scr32"], writes=["aoT"])
                deferred[0] = [fin_pair]
        flush_deferred()

        for tt in range(4):
            trow = q0 + tt * 128
            wps, wpn = (SA, "SA") if tt % 2 == 0 else (SA2, "SA2")
            fns = []
            for cc in range(8):
                for half in range(2):
                    def fo(c=cc, hf=half, t=tt, p=wps):
                        def f(e):
                            return e.matmul(p[:, hf * 512:(hf + 1) * 512],
                                            aoT[:, c, t * 128:(t + 1) * 128],
                                            wot_sb[:, c, hf * 512:(hf + 1) * 512],
                                            start=(c == 0), stop=(c == 7), skip_group_check=True)
                        return f
                    fns.append(fo())
            group("tensor", fns, reads=["wot", "aoT"], writes=[wpn])
            pi = tt % 2
            xn, ln_ = "xp%d" % pi, "lnp%d" % pi
            op("sync", (lambda r, p: lambda e: e.dma_start(out=xp[p][:], in_=x_d[r:r + 128, :]))(trow, pi),
               writes=[xn], dma_sem=sSx[pi])
            for ch in range(2):
                op("vector", (lambda p, cch, wq=wps: lambda e: e.scalar_tensor_tensor(
                    scr32[:], wq[:, cch * 512:(cch + 1) * 512], 1.0,
                    xp[p][:, cch * 512:(cch + 1) * 512],
                    AluOpType.mult, AluOpType.add))(pi, ch),
                   reads=[wpn, xn], writes=["scr32"])
                op("vector", (lambda p, cch: lambda e: e.tensor_tensor(
                    xp[p][:, cch * 512:(cch + 1) * 512], scr32[:],
                    bo_b[:, cch * 512:(cch + 1) * 512], AluOpType.add))(pi, ch),
                   reads=["scr32", "bo_b"], writes=[xn])
            op("sync", (lambda r, p: lambda e: e.dma_start(out=y_d[r:r + 128, :], in_=xp[p][:]))(trow, pi),
               reads=[xn], writes=["yb_%d_%d" % (qb, tt)], dma_sem=sSy)
            ln_full(pi, xn, xp[pi][:], ln_, lnp[pi][:])
            op("sync", (lambda t, p: lambda e: e.dma_start_transpose(
                ln2T[:, :, t * 128:(t + 1) * 128], lnp[p][:]))(tt, pi),
               reads=[ln_], writes=["ln2Tbuf"] if tt == 3 else [], dma_sem=sT2)
            if tt < 3:
                G.B("ln2Tbuf").did_write((sT2, sT2.v))
        if dbg and qb == 0:
            tap(["aoT"], t_ao, aoT[:].rearrange("p a b -> p (a b)"))
            tap(["ln2Tbuf"], t_ln2, ln2T[:].rearrange("p a b -> p (a b)"))

        G.ops["scalar"].append((load_table(SILU_ID), [], None))
        if qb + 1 < NW:
            _units = dpre_units(qb + 1)
            nbm = 2 * 8
            u_bm, u_dchain, u_aug = _units[0:nbm], _units[nbm], _units[nbm + 1:]
            assert len(u_aug) == 8
        else:
            u_bm, u_dchain, u_aug = [], None, []

        def ld_wgu(c):
            b = c % 3
            op("gpsimd", (lambda cx, bx: lambda e: e.dma_start(
                out=wgb[bx][:], in_=wg_d[:, cx * 128:(cx + 1) * 128].rearrange("(cc p) f -> p cc f", p=128)))(c, b),
               writes=["wgb%d" % b], dma_sem=sWg)
            op("gpsimd", (lambda cx, bx: lambda e: e.dma_start(
                out=wub[bx][:], in_=wu_d[:, cx * 128:(cx + 1) * 128].rearrange("(cc p) f -> p cc f", p=128)))(c, b),
               writes=["wub%d" % b], dma_sem=sWu)
        ld_wgu(0)
        ld_wgu(1)
        for hc in range(NHC):
            b = hc % 3
            if hc + 2 < NHC:
                ld_wgu(hc + 2)
            if 3 <= hc < 12:
                while u_bm and len(u_bm) > 16 * (11 - hc) // 9:
                    u_bm.pop(0)()
            if hc == 12 and u_dchain is not None:
                G.ops["scalar"].append((load_table(NLX_ID), [], None))
                u_dchain()
                G.ops["scalar"].append((load_table(SILU_ID), [], None))
            if hc >= 12:
                while u_aug and len(u_aug) > 8 * (NHC - 1 - hc) // 12:
                    u_aug.pop(0)()
            group("tensor", [
                (lambda c, bx: lambda e: e.matmul(gps[:, 0:512], wgb[bx][:, c, :], ln2T[:, c, :],
                                                  start=(c == 0), stop=(c == 7), skip_group_check=True))(cc, b)
                for cc in range(8)], reads=["wgb%d" % b, "ln2Tbuf"], writes=["gps"])
            group("tensor", [
                (lambda c, bx: lambda e: e.matmul(ups[:, 0:512], wub[bx][:, c, :], ln2T[:, c, :],
                                                  start=(c == 0), stop=(c == 7), skip_group_check=True))(cc, b)
                for cc in range(8)], reads=["wub%d" % b, "ln2Tbuf"], writes=["ups"])
            op("scalar", lambda e: e.activation(t1[:], gps[:, 0:512], AF.Silu, scale=QS),
               reads=["gps"], writes=["t1"])
            op("vector", lambda e: e.tensor_tensor(hch, t1[:], ups[:, 0:512], AluOpType.mult),
               reads=["t1", "ups"], writes=["scrbf"])
            if dbg and qb == 0 and hc == 0:
                tap(["scrbf"], t_h, hch)
            op("gpsimd", (lambda c, q: lambda e: e.dma_start(
                out=hts_d[c * 128:(c + 1) * 128, q:q + WQ], in_=hch))(hc, q0),
               reads=["scrbf"], writes=["htsbuf"] if hc == NHC - 1 else [], dma_sem=sHw[qb])
            if hc < NHC - 1:
                G.B("htsbuf").did_write((sHw[qb], sHw[qb].v))
        G.ops["scalar"].append((load_table(NLX_ID), [], None))
        hw_marks[qb] = (sHw[qb], sHw[qb].v)

    wdw = {}
    for nm in ("qqh", "k1Tb", "k2Tb", "nq0", "nq1", "wq1w", "wq2w", "wk1w",
               "lnt0", "lnt1", "dgb", "nkpb"):
        for s, v in G.B(nm).write_waits().items():
            wdw[s] = max(wdw.get(s, 0), v)
    op("gpsimd", lambda e: e.dma_start(
        out=pbuf[:, 0:NHC * C].rearrange("p (hc f) -> p hc f", hc=NHC),
        in_=wd_d.rearrange("(hc p) f -> p hc f", p=128)),
       writes=["wdT"], dma_sem=sGw, extra_waits=list(wdw.items()))

    hTt2 = ln2T[:].rearrange("p a b -> p (a b)")[:, 0:NHC * 128].rearrange(
        "p (hc t) -> p hc t", hc=NHC)
    for tq in range(NTT):
        trow = tq * 128
        htv = [hTt, hTt2][tq % 2]
        htn = ["aoT", "ln2Tbuf"][tq % 2]
        op("gpsimd", (lambda r, hv: lambda e: e.dma_start(
            out=hv, in_=hts_d[:, r:r + 128].rearrange("(hc p) t -> p hc t", p=128)))(trow, htv),
           reads=["htsbuf"], writes=[htn], dma_sem=sHl,
           extra_waits=[hw_marks[tq // 4]])
        pi = tq % 2
        xn = "xp%d" % pi
        op("sync", (lambda r, p: lambda e: e.dma_start(out=xp[p][:], in_=y_d[r:r + 128, :]))(trow, pi),
           writes=[xn], dma_sem=sSx[pi],
           extra_waits=[(sSy, sSy.v)])
        if dbg and tq == 0:
            tap([xn], t_x2r, xp[pi][:])
            tap(["aoT"], t_hT, hTt.rearrange("p a b -> p (a b)"))
        for ch in range(2):
            group("tensor", [
                (lambda c, cch, hv: lambda e: e.matmul(
                    SA2[:, 0:512], hv[:, c, :], wdT(c, cch * 512, (cch + 1) * 512),
                    start=(c == 0), stop=(c == NHC - 1), skip_group_check=True))(hc, ch, htv)
                for hc in range(NHC)], reads=[htn, "wdT"], writes=["SA2"])
            op("vector", (lambda cch, p: lambda e: e.scalar_tensor_tensor(
                scr32[:], SA2[:, 0:512], QS2,
                xp[p][:, cch * 512:(cch + 1) * 512],
                AluOpType.mult, AluOpType.add))(ch, pi),
               reads=["SA2", xn], writes=["scr32"])
            if dbg and tq == 0:
                tap(["scr32"], t_f0 if ch == 0 else t_f1, scr32[:])
            op("sync", (lambda r, cch: lambda e: e.dma_start(
                out=y_d[r:r + 128, cch * 512:(cch + 1) * 512], in_=scr32[:]))(trow, ch),
               reads=["scr32"], writes=["yf_%d_%d" % (tq, ch)], dma_sem=sSy)

    G.wait_only("sync", {sSy: sSy.v})
    G.wait_only("gpsimd", {sHl: sHl.v, sGw: sGw.v, sTap: sTap.v})

    with nc.Block() as block:
        def emitter(engname):
            def emit(eng):
                last = {}
                for fn, waits, inc in G.ops[engname]:
                    for s, v in waits:
                        if last.get(s.name, 0) < v:
                            eng.wait_ge(s.h, v)
                            last[s.name] = v
                    ins = fn(eng) if fn is not None else None
                    if inc is not None and ins is not None:
                        s, amt = inc
                        ins.then_inc(s.h, amt)
            return emit
        block.tensor(emitter("tensor"))
        block.scalar(emitter("scalar"))
        block.vector(emitter("vector"))
        block.gpsimd(emitter("gpsimd"))
        block.sync(emitter("sync"))

    es.close()
    return nc


_NC_CACHE = {}


def _get_nc():
    if "nc" not in _NC_CACHE:
        _NC_CACHE["nc"] = build_nc()
    return _NC_CACHE["nc"]


def make_consts():
    ident = np.eye(128, dtype=np.float32)
    mask2 = (-1e9 * np.tril(np.ones((128, 128), np.float32), -1)).astype(np.float32)
    maskqm = (-1e9 * np.triu(np.ones((128, 128), np.float32), 1)).astype(np.float32)
    rep1a = np.zeros((80, 8, 128), np.float32)
    rep2 = np.zeros((128, 8, 128), np.float32)
    repadp = np.zeros((128, 8, 128), np.float32)
    for g8 in range(8):
        for j2 in range(2):
            hh = 2 * g8 + j2
            for r in range(R1):
                for s in range(R2):
                    rep1a[R1 * hh + r, g8, 64 * j2 + R2 * r + s] = 1.0
                    rep2[R2 * hh + s, g8, 64 * j2 + R2 * r + s] = 1.0
            rep1a[64 + hh, g8, 64 * j2 + 32] = 1.0
        repadp[2 * g8, g8, 32] = 1.0
        repadp[2 * g8 + 1, g8, 96] = 1.0
    bd1 = np.zeros((128, 16), np.float32)
    bd2 = np.zeros((128, 16), np.float32)
    for hh in range(16):
        bd1[R1 * hh:R1 * hh + R1, hh] = 1.0
        bd2[R2 * hh:R2 * hh + R2, hh] = 1.0
    return dict(ident=ident, identb=ident.copy(), mask2=mask2, maskqm=maskqm,
                rep1a=rep1a, rep2=rep2, repadp=repadp, bd1=bd1, bd2=bd2)


def kernel(L0_attn_x, L0_attn_Wq1, L0_attn_Wk1, L0_attn_Wq2, L0_attn_Wk2,
           L0_attn_Wv, L0_attn_Wo, L0_attn_bo, L0_ffn_w_gate, L0_ffn_w_up,
           L0_ffn_w_down):
    x = np.ascontiguousarray(np.asarray(L0_attn_x, dtype=np.float32))
    B = x.shape[0]
    shared = dict(
        wq1t=np.ascontiguousarray(np.asarray(L0_attn_Wq1, np.float32).T),
        wk1t=np.ascontiguousarray(np.asarray(L0_attn_Wk1, np.float32).T),
        wq2t=np.ascontiguousarray(np.asarray(L0_attn_Wq2, np.float32).T),
        wk2t=np.ascontiguousarray(np.asarray(L0_attn_Wk2, np.float32).T),
        wvt=np.ascontiguousarray(np.asarray(L0_attn_Wv, np.float32).T),
        wot=np.ascontiguousarray(np.asarray(L0_attn_Wo, np.float32).T),
        bo=np.ascontiguousarray(np.asarray(L0_attn_bo, np.float32).reshape(1, C)),
        wgt=np.ascontiguousarray(np.asarray(L0_ffn_w_gate, np.int32).T),
        wut=np.ascontiguousarray(np.asarray(L0_ffn_w_up, np.int32).T),
        wdt=np.ascontiguousarray(np.asarray(L0_ffn_w_down, np.int32).T),
        **make_consts())
    nc = _get_nc()
    in_maps = [dict(x=np.ascontiguousarray(x[b]), **shared) for b in range(B)]
    res = run_bass_kernel_spmd(nc, in_maps, core_ids=list(range(B)))
    out = np.stack([res.results[b]["y"] for b in range(B)], axis=0)
    return out.astype(np.float32)


# version 17
# speedup vs baseline: 1.0856x; 1.0856x over previous
import contextlib
import numpy as np
import concourse.bass as bass
import concourse.mybir as mybir
from concourse.bass_utils import run_bass_kernel_spmd
from concourse.alu_op_type import AluOpType

F32 = mybir.dt.float32
BF16 = mybir.dt.bfloat16
I32 = mybir.dt.int32
AF = mybir.ActivationFunctionType

N = 2048
C = 1024
H = 16
HD = 64
R1, R2 = 4, 8
SCALE = float(32 ** -0.5)
C1 = float(np.float32(72.0 / SCALE))
C2 = float(np.float32(45.0 / SCALE))
TWIN = float(np.float32(C1 + C2))
DFLOOR = -400.0
QS = float(np.float32(1.0 / 127.0))
QS2 = float(np.float32(QS) * np.float32(QS))
HID = 3072
NHC = HID // 128
NW = 4
WQ = 512
NTT = N // 128
EPS = 1e-5


class Sem:
    def __init__(self, nc, es, name):
        self.h = es.enter_context(nc.semaphore(name))
        self.v = 0
        self.name = name


class Buf:
    def __init__(self):
        self.w = {}
        self.r = {}

    def read_waits(self):
        return dict(self.w)

    def write_waits(self):
        m = dict(self.w)
        for s, v in self.r.items():
            m[s] = max(m.get(s, 0), v)
        return m

    def did_read(self, mk):
        if mk:
            s, v = mk
            self.r[s] = max(self.r.get(s, 0), v)

    def did_write(self, mk):
        self.w = {}
        self.r = {}
        if mk:
            s, v = mk
            self.w[s] = v


class Gen:
    ENGS = ("tensor", "scalar", "vector", "gpsimd", "sync")

    def __init__(self, nc, es):
        self.nc = nc
        self.ops = {e: [] for e in self.ENGS}
        self.bufs = {}
        self.eng_sem = {e: Sem(nc, es, "sem_" + e) for e in self.ENGS}

    def B(self, name):
        if name not in self.bufs:
            self.bufs[name] = Buf()
        return self.bufs[name]

    def group(self, eng, fns, reads=(), writes=(), extra_waits=(), dma_sem=None):
        waits = {}
        for nm in reads:
            for s, v in self.B(nm).read_waits().items():
                waits[s] = max(waits.get(s, 0), v)
        for nm in writes:
            for s, v in self.B(nm).write_waits().items():
                waits[s] = max(waits.get(s, 0), v)
        for s, v in extra_waits:
            waits[s] = max(waits.get(s, 0), v)
        for s, v in waits.items():
            assert v <= s.v, f"wait {s.name}>={v} but scheduled only {s.v}"
        sem = dma_sem if dma_sem is not None else self.eng_sem[eng]
        amt = 16 if dma_sem is not None else 1
        mk = None
        n = len(fns)
        for i, fn in enumerate(fns):
            w = list(waits.items()) if i == 0 else []
            inc = None
            if i == n - 1:
                sem.v += amt
                inc = (sem, amt)
                mk = (sem, sem.v)
            self.ops[eng].append((fn, w, inc))
        for nm in reads:
            self.B(nm).did_read(mk)
        for nm in writes:
            self.B(nm).did_write(mk)
        return mk

    def op(self, eng, fn, **kw):
        return self.group(eng, [fn], **kw)

    def wait_only(self, eng, waits):
        self.ops[eng].append((None, list(waits.items()), None))


def build_nc(dbg=False):
    nc = bass.Bass("TRN2", target_bir_lowering=False, debug=False)
    es = contextlib.ExitStack()

    x_d = nc.dram_tensor("x", [N, C], F32, kind="ExternalInput").ap()
    y_d = nc.dram_tensor("y", [N, C], F32, kind="ExternalOutput").ap()
    wq1_d = nc.dram_tensor("wq1t", [C, H * R1], F32, kind="ExternalInput").ap()
    wq2_d = nc.dram_tensor("wq2t", [C, H * R2], F32, kind="ExternalInput").ap()
    wk1_d = nc.dram_tensor("wk1t", [C, H * R1], F32, kind="ExternalInput").ap()
    wk2_d = nc.dram_tensor("wk2t", [C, H * R2], F32, kind="ExternalInput").ap()
    wv_d = nc.dram_tensor("wvt", [C, C], F32, kind="ExternalInput").ap()
    wo_d = nc.dram_tensor("wot", [C, C], F32, kind="ExternalInput").ap()
    bo_d = nc.dram_tensor("bo", [1, C], F32, kind="ExternalInput").ap()
    wg_d = nc.dram_tensor("wgt", [C, HID], I32, kind="ExternalInput").ap()
    wu_d = nc.dram_tensor("wut", [C, HID], I32, kind="ExternalInput").ap()
    wd_d = nc.dram_tensor("wdt", [HID, C], I32, kind="ExternalInput").ap()
    ident_d = nc.dram_tensor("ident", [128, 128], F32, kind="ExternalInput").ap()
    identb_d = nc.dram_tensor("identb", [128, 128], F32, kind="ExternalInput").ap()
    mask2_d = nc.dram_tensor("mask2", [128, 128], F32, kind="ExternalInput").ap()
    maskqm_d = nc.dram_tensor("maskqm", [128, 128], F32, kind="ExternalInput").ap()
    rep1a_d = nc.dram_tensor("rep1a", [80, 8, 128], F32, kind="ExternalInput").ap()
    rep2_d = nc.dram_tensor("rep2", [128, 8, 128], F32, kind="ExternalInput").ap()
    repadp_d = nc.dram_tensor("repadp", [128, 8, 128], F32, kind="ExternalInput").ap()
    bd1_d = nc.dram_tensor("bd1", [128, 16], F32, kind="ExternalInput").ap()
    bd2_d = nc.dram_tensor("bd2", [128, 16], F32, kind="ExternalInput").ap()
    hts_d = nc.dram_tensor("hts", [HID, N], BF16, kind="Internal").ap()
    if dbg:
        t_q1x = nc.dram_tensor("t_q1x", [80, N], F32, kind="ExternalOutput").ap()
        t_q2T = nc.dram_tensor("t_q2T", [128, N], F32, kind="ExternalOutput").ap()
        t_dg = nc.dram_tensor("t_dg", [16, N], F32, kind="ExternalOutput").ap()
        t_nkp = nc.dram_tensor("t_nkp", [16, N], F32, kind="ExternalOutput").ap()
        t_kk = nc.dram_tensor("t_kk", [128, 8 * N], F32, kind="ExternalOutput").ap()
        t_nq = nc.dram_tensor("t_nq", [16, 2 * N], F32, kind="ExternalOutput").ap()
        t_v = nc.dram_tensor("t_v", [128, NTT * H * (HD + 1)], F32, kind="ExternalOutput").ap()
        t_qq = nc.dram_tensor("t_qq", [128, 8 * WQ], F32, kind="ExternalOutput").ap()
        t_E = nc.dram_tensor("t_E", [128, 1024], F32, kind="ExternalOutput").ap()
        t_rr = nc.dram_tensor("t_rr", [64, WQ], F32, kind="ExternalOutput").ap()
        t_ao = nc.dram_tensor("t_ao", [128, 8 * WQ], F32, kind="ExternalOutput").ap()
        t_ln2 = nc.dram_tensor("t_ln2", [128, 8 * WQ], F32, kind="ExternalOutput").ap()
        t_h = nc.dram_tensor("t_h", [128, WQ], F32, kind="ExternalOutput").ap()
        t_x2r = nc.dram_tensor("t_x2r", [128, C], F32, kind="ExternalOutput").ap()
        t_hT = nc.dram_tensor("t_hT", [128, NHC * 128], F32, kind="ExternalOutput").ap()
        t_f0 = nc.dram_tensor("t_f0", [128, WQ], F32, kind="ExternalOutput").ap()
        t_f1 = nc.dram_tensor("t_f1", [128, WQ], F32, kind="ExternalOutput").ap()

    A = lambda name, shape, dt: nc.alloc_sbuf_tensor(name, shape, dt)
    v_sb = A("v_sb", [128, NTT, H, HD + 1], BF16)
    kk_sb = A("kk_sb", [128, 8, N], BF16)
    q1x = A("q1x", [80, N], BF16)
    q2T = A("q2T", [128, N], BF16)
    ident_sb = A("ident_sb", [128, 128], F32)
    identb_sb = A("identb_sb", [128, 128], BF16)
    mask2_sb = A("mask2_sb", [128, 128], BF16)
    maskqm_sb = A("maskqm_sb", [128, 128], BF16)
    Lmq = A("Lmq", [128, 4, 16], F32)
    Dw = A("Dw", [16, 1024], F32)
    Dfull = A("Dfull", [128, 512], BF16)
    rrpad = A("rrpad", [128, 1024], BF16)
    onesel = A("onesel", [128, 64], BF16)
    rep1a_sb = A("rep1a_sb", [80, 8, 128], BF16)
    rep2_sb = A("rep2_sb", [128, 8, 128], BF16)
    repadp_sb = A("repadp_sb", [128, 8, 128], BF16)
    bd1_sb = A("bd1_sb", [128, 16], BF16)
    bd2_sb = A("bd2_sb", [128, 16], BF16)
    ones128 = A("ones128", [1, 128], BF16)
    bo_bf = A("bo_bf", [1, C], BF16)
    xp = [A("xp0", [128, C], F32), A("xp1", [128, C], F32)]
    lnp = [A("lnp0", [128, C], BF16), A("lnp1", [128, C], BF16)]
    statsv = [A("stats0", [128, 8], F32), A("stats1", [128, 8], F32)]
    statc = A("statc", [128, 3], F32)
    epsb = statc[:, 0:1]
    tinyb = statc[:, 1:2]
    sh20 = statc[:, 2:3]
    scrbf = A("scrbf", [128, WQ], BF16)
    sqt = scrbf[:, 384:512]
    lnt4_off = (nc.sbuf_base + 31) // 32 * 32
    lnt4 = A("lnt4", [128, 8, 512], BF16)
    wgi = [nc.alloc_sbuf_tensor_at("wgi%d" % i, [128, 8, 128], I32,
                                   offset=lnt4_off + i * 4096) for i in range(2)]
    rrb = rrpad[0:1, 0:WQ]
    hch = scrbf[:, 0:WQ]
    scr32 = A("scr32", [128, WQ], F32)
    lnr = scr32[0:1, :]
    rrs = scr32[64:128, :]
    ntmp = scr32[0:16, :]
    t1 = A("t1", [128, WQ], BF16)
    aoT = A("aoT", [128, 8, WQ], BF16)
    hTt = aoT[:].rearrange("p a b -> p (a b)")[:, 0:NHC * 128].rearrange(
        "p (hc t) -> p hc t", hc=NHC)
    ln2T = A("ln2T", [128, 8, WQ], BF16)
    wgb = [A("wgb%d" % i, [128, 8, 128], BF16) for i in range(3)]
    wub = [A("wub%d" % i, [128, 8, 128], BF16) for i in range(3)]
    wot_sb = A("wot_sb", [128, 8, C], BF16)
    pbuf = A("pbuf", [128, 25600], BF16)

    PB_WV = 0
    PB_K1 = 8192
    PB_K2 = 10240
    PB_NQ1 = 12288
    PB_NQ2 = 14336
    PB_DG = 16384
    PB_NKP = 18432
    PB_LT0 = 20480
    PB_LT1 = 21504
    PB_WQ1 = 22528
    PB_WQ2 = 23040
    PB_WK1 = 24064
    PB_WK2 = 24576

    def wvT(cc, f0, f1):
        return pbuf[:, PB_WV + cc * C + f0: PB_WV + cc * C + f1]

    def k1T(t0, t1):
        return pbuf[0:64, PB_K1 + t0: PB_K1 + t1]

    def k2T(t0, t1):
        return pbuf[0:128, PB_K2 + t0: PB_K2 + t1]

    def nq(i, t0, t1):
        off = PB_NQ1 if i == 0 else PB_NQ2
        return pbuf[0:16, off + t0: off + t1]

    def dg(t0, t1):
        return pbuf[0:16, PB_DG + t0: PB_DG + t1]

    def nkp(t0, t1):
        return pbuf[0:16, PB_NKP + t0: PB_NKP + t1]

    def ping2(t0, t1):
        return pbuf[0:16, PB_NQ2 + t0: PB_NQ2 + t1]

    def wq1w(cc):
        return pbuf[:, PB_WQ1 + cc * 64: PB_WQ1 + (cc + 1) * 64]

    def wq2w(cc):
        return pbuf[:, PB_WQ2 + cc * 128: PB_WQ2 + (cc + 1) * 128]

    def wk1w(cc):
        return pbuf[:, PB_WK1 + cc * 64: PB_WK1 + (cc + 1) * 64]

    def wk2w(cc):
        return pbuf[:, PB_WK2 + cc * 128: PB_WK2 + (cc + 1) * 128]

    def lntp(p):
        off = PB_LT0 if p == 0 else PB_LT1
        return pbuf[:, off: off + 1024].rearrange("p (cc t) -> p cc t", cc=8)

    def wdT(hc, f0, f1):
        return pbuf[:, hc * C + f0: hc * C + f1]

    Eb1 = pbuf[:, PB_WK2:PB_WK2 + 1024]

    def qqh(hh):
        return pbuf[:, PB_WV + hh * 512: PB_WV + (hh + 1) * 512]

    Eb0 = pbuf[:, PB_K1:PB_K1 + 1024]
    bo_b = pbuf[:, PB_K1 + 1024:PB_K1 + 2048]
    lnsq = [pbuf[:, PB_LT0:PB_LT0 + 512], pbuf[:, PB_LT0 + 512:PB_LT0 + 1024]]
    sqtb = pbuf[:, PB_LT1:PB_LT1 + 512]

    SA = es.enter_context(nc.psum_tensor("SA", [128, 1024], F32))
    SA2 = es.enter_context(nc.psum_tensor("SA2", [128, 1024], F32))
    accb = [es.enter_context(nc.psum_tensor("acc0", [128, 512], F32)),
            es.enter_context(nc.psum_tensor("acc1", [128, 512], F32))]
    gps = es.enter_context(nc.psum_tensor("gps", [128, 512], F32))
    ups = es.enter_context(nc.psum_tensor("ups", [128, 512], F32))

    G = Gen(nc, es)
    sGw = Sem(nc, es, "sGw")
    sSx = [Sem(nc, es, "sSx0"), Sem(nc, es, "sSx1")]
    sSy = Sem(nc, es, "sSy")
    sTp = [Sem(nc, es, "sTp0"), Sem(nc, es, "sTp1")]
    sT2 = Sem(nc, es, "sT2")
    sWg = Sem(nc, es, "sWg")
    sWu = Sem(nc, es, "sWu")
    sHw = [Sem(nc, es, "sHw%d" % i) for i in range(NW)]
    sHl = Sem(nc, es, "sHl")
    sTap = Sem(nc, es, "sTap")

    from concourse.hw_specs import get_activation_tables
    tabs = list(get_activation_tables(nc.m.arch).items())

    def set_with(*fs):
        for i, (nm, funcs) in enumerate(tabs):
            if all(f in funcs for f in fs):
                return i
        raise ValueError(f"no activation table set with {fs}")
    NLX_ID = set_with(AF.Exp, AF.Ln)
    SILU_ID = set_with(AF.Silu)

    def load_table(tid):
        def f(e):
            return e.add_instruction(mybir.InstLoadActFuncSet(
                name=nc.get_next_instruction_name(), act_func_set_id=tid,
                ins=[], outs=[]))
        return f

    op, group = G.op, G.group

    def tap(buf_names, out_ap, in_ap):
        if not dbg:
            return
        def f(e):
            with nc.allow_non_contiguous_dma(reason="debug tap"):
                return e.dma_start(out=out_ap, in_=in_ap)
        op("gpsimd", f, reads=list(buf_names), dma_sem=sTap)

    G.ops["scalar"].append((load_table(NLX_ID), [], None))
    op("gpsimd", lambda e: e.memset(v_sb[:, :, :, HD:HD + 1], 1.0), writes=["vones"])
    op("gpsimd", lambda e: e.memset(ones128[:], 1.0), writes=["ones128"])
    op("vector", lambda e: e.memset(epsb, EPS), writes=["statc"])
    op("vector", lambda e: e.memset(tinyb, 1e-30), writes=["statc"])
    op("vector", lambda e: e.memset(sh20, float(20.0 * np.log(2.0))), writes=["statc"])
    op("gpsimd", lambda e: e.memset(kk_sb[:], 0.0), writes=["kk"])
    op("gpsimd", lambda e: e.memset(scrbf[:], 0.0), writes=["scrbf"])
    op("gpsimd", lambda e: e.memset(pbuf[:, PB_LT1:PB_LT1 + 512], 0.0), writes=["sqtb"])
    op("gpsimd", lambda e: e.memset(q1x[64:80, :], 0.0), writes=["q1x"])
    op("gpsimd", lambda e: e.memset(Dfull[:], 0.0), writes=["Dfull"])
    op("gpsimd", lambda e: e.memset(rrpad[:], 0.0), writes=["scrbf_rr"])
    op("gpsimd", lambda e: e.memset(onesel[:], 0.0), writes=["onesel"])
    op("gpsimd", lambda e: e.memset(onesel[0:1, :], 1.0), writes=["onesel"])
    op("gpsimd", lambda e: e.memset(pbuf[64:80, PB_K1:PB_K1 + N], -1.0), writes=["k1Tb"])
    for nm, dst, srcd in [("ident", ident_sb[:], ident_d), ("identb", identb_sb[:], identb_d),
                          ("mask2", mask2_sb[:], mask2_d), ("maskqm", maskqm_sb[:], maskqm_d),
                          ("rep1a", rep1a_sb[:], rep1a_d), ("rep2", rep2_sb[:], rep2_d),
                          ("repadp", repadp_sb[:], repadp_d),
                          ("bd1", bd1_sb[:], bd1_d), ("bd2", bd2_sb[:], bd2_d),
                          ("bo_bf", bo_bf[:], bo_d)]:
        op("gpsimd", (lambda d, s: lambda e: e.dma_start(out=d, in_=s))(dst, srcd),
           writes=[nm], dma_sem=sGw)
    op("gpsimd", lambda e: e.dma_start(
        out=pbuf[:, PB_WV:PB_WV + 8 * C].rearrange("p (cc f) -> p cc f", cc=8),
        in_=wv_d.rearrange("(cc p) f -> p cc f", p=128)), writes=["wvT"], dma_sem=sGw)
    op("gpsimd", lambda e: e.dma_start(
        out=pbuf[:, PB_WQ1:PB_WQ1 + 512].rearrange("p (cc f) -> p cc f", cc=8),
        in_=wq1_d.rearrange("(cc p) f -> p cc f", p=128)), writes=["wq1w"], dma_sem=sGw)
    op("gpsimd", lambda e: e.dma_start(
        out=pbuf[:, PB_WQ2:PB_WQ2 + 1024].rearrange("p (cc f) -> p cc f", cc=8),
        in_=wq2_d.rearrange("(cc p) f -> p cc f", p=128)), writes=["wq2w"], dma_sem=sGw)
    op("gpsimd", lambda e: e.dma_start(
        out=pbuf[:, PB_WK1:PB_WK1 + 512].rearrange("p (cc f) -> p cc f", cc=8),
        in_=wk1_d.rearrange("(cc p) f -> p cc f", p=128)), writes=["wk1w"], dma_sem=sGw)
    op("gpsimd", lambda e: e.dma_start(
        out=pbuf[:, PB_WK2:PB_WK2 + 1024].rearrange("p (cc f) -> p cc f", cc=8),
        in_=wk2_d.rearrange("(cc p) f -> p cc f", p=128)), writes=["wk2w"], dma_sem=sGw)
    op("gpsimd", lambda e: e.dma_start(out=wot_sb[:], in_=wo_d.rearrange("(cc p) f -> p cc f", p=128)),
       writes=["wot"], dma_sem=sGw)
    for _nm in ("ident", "identb", "mask2", "maskqm", "rep1a", "rep2", "repadp",
                "bd1", "bd2", "bo_bf", "wvT", "wq1w", "wq2w", "wk1w", "wk2w", "wot"):
        G.B(_nm).did_write((sGw, sGw.v))

    def ln_full(p, xname, x_ap, outname, out_ap):
        st = statsv[p]
        sn = "stats%d" % p
        s1, s2, s2b = st[:, 0:1], st[:, 1:2], st[:, 2:3]
        mst, vst, rstd, nmr = st[:, 3:4], st[:, 4:5], st[:, 5:6], st[:, 6:7]
        sq = lnsq[p]
        op("vector", (lambda a=s1: lambda e: e.reduce_sum(a, x_ap, axis=mybir.AxisListType.X))(),
           reads=[xname], writes=[sn])
        op("scalar", (lambda a=s2, q=sq: lambda e: e.activation(
            q, x_ap[:, 0:512], AF.Square, accum_out=a))(),
           reads=[xname], writes=[sn, "lnsq%d" % p])
        op("scalar", (lambda a=s2b, q=sq: lambda e: e.activation(
            q, x_ap[:, 512:1024], AF.Square, accum_out=a))(),
           reads=[xname], writes=[sn, "lnsq%d" % p])
        op("vector", (lambda a=s2, b=s2b: lambda e: e.tensor_tensor(a, a, b, AluOpType.add))(),
           reads=[sn], writes=[sn])
        op("vector", (lambda a=mst, b=s1: lambda e: e.tensor_scalar_mul(a, b, 1.0 / C))(),
           reads=[sn], writes=[sn])
        op("vector", (lambda a=nmr, b=mst: lambda e: e.tensor_tensor(a, b, b, AluOpType.mult))(),
           reads=[sn], writes=[sn])
        op("vector", (lambda a=vst, b=s2: lambda e: e.tensor_scalar_mul(a, b, 1.0 / C))(),
           reads=[sn], writes=[sn])
        op("vector", (lambda a=vst, b=nmr: lambda e: e.tensor_tensor(a, a, b, AluOpType.subtract))(),
           reads=[sn], writes=[sn])
        op("scalar", (lambda a=vst: lambda e: e.activation(a, a, AF.Ln, bias=epsb))(),
           reads=[sn, "statc"], writes=[sn])
        op("scalar", (lambda a=rstd, b=vst: lambda e: e.activation(a, b, AF.Exp, scale=-0.5))(),
           reads=[sn], writes=[sn])
        op("vector", (lambda a=nmr, b=mst, c=rstd: lambda e: e.tensor_scalar(
            a, b, c, -1.0, AluOpType.mult, AluOpType.mult))(),
           reads=[sn], writes=[sn])
        op("vector", (lambda c=rstd, d=nmr: lambda e: e.tensor_scalar(
            out_ap, x_ap, c, d, AluOpType.mult, AluOpType.add))(),
           reads=[xname, sn], writes=[outname])

    op("sync", lambda e: e.dma_start(out=xp[0][:], in_=x_d[0:128, :]),
       writes=["xp0"], dma_sem=sSx[0])
    for vol in range(4):
        vs, ve = vol * 512, (vol + 1) * 512
        for t4 in range(4):
            tt = 4 * vol + t4
            pi = tt % 2
            xn, ln_ = "xp%d" % pi, "lnp%d" % pi
            if tt + 1 < NTT:
                op("sync", (lambda t, p: lambda e: e.dma_start(
                    out=xp[p][:], in_=x_d[t * 128:(t + 1) * 128, :]))(tt + 1, (tt + 1) % 2),
                   writes=["xp%d" % ((tt + 1) % 2)], dma_sem=sSx[(tt + 1) % 2])
            ln_full(pi, xn, xp[pi][:], ln_, lnp[pi][:])
            op("sync", (lambda p, t4_: lambda e: e.dma_start_transpose(
                lnt4[:, :, t4_ * 128:(t4_ + 1) * 128], lnp[p][:]))(pi, t4),
               reads=[ln_], writes=["lnt4"], dma_sem=sTp[pi])
        group("tensor", [
            (lambda c: lambda e: e.matmul(gps[0:64, 0:512], wq1w(c), lnt4[:, c, :],
                                          start=(c == 0), stop=(c == 7), skip_group_check=True))(cc)
            for cc in range(8)], reads=["wq1w", "lnt4"], writes=["gps"])
        group("tensor", [
            (lambda c: lambda e: e.matmul(ups[:, 0:512], wq2w(c), lnt4[:, c, :],
                                          start=(c == 0), stop=(c == 7), skip_group_check=True))(cc)
            for cc in range(8)], reads=["wq2w", "lnt4"], writes=["ups"])
        group("tensor", [
            (lambda c: lambda e: e.matmul(SA[0:64, 0:512], wk1w(c), lnt4[:, c, :],
                                          start=(c == 0), stop=(c == 7), skip_group_check=True))(cc)
            for cc in range(8)], reads=["wk1w", "lnt4"], writes=["SA"])
        group("tensor", [
            (lambda c: lambda e: e.matmul(SA[:, 512:1024], wk2w(c), lnt4[:, c, :],
                                          start=(c == 0), stop=(c == 7), skip_group_check=True))(cc)
            for cc in range(8)], reads=["wk2w", "lnt4"], writes=["SA"])
        op("vector", (lambda a, b: lambda e: e.tensor_copy(q1x[0:64, a:b], gps[0:64, 0:512]))(vs, ve),
           reads=["gps"], writes=["q1x"])
        op("vector", (lambda a, b: lambda e: e.tensor_copy(q2T[:, a:b], ups[:, 0:512]))(vs, ve),
           reads=["ups"], writes=["q2T"])
        op("vector", (lambda a, b: lambda e: e.tensor_copy(k1T(a, b), SA[0:64, 0:512]))(vs, ve),
           reads=["SA"], writes=["k1Tb"])
        op("vector", (lambda a, b: lambda e: e.tensor_copy(k2T(a, b), SA[:, 512:1024]))(vs, ve),
           reads=["SA"], writes=["k2Tb"])
        for t4 in range(4):
            tt = 4 * vol + t4
            if t4 % 2 == 0:
                group("tensor", [
                    (lambda c, hf, t_: lambda e: e.matmul(
                        SA2[:, hf * 512:(hf + 1) * 512], lnt4[:, c, t_ * 128:(t_ + 1) * 128],
                        wvT(c, hf * 512, (hf + 1) * 512),
                        start=(c == 0), stop=(c == 7), skip_group_check=True))(cc, half, t4)
                    for half in range(2) for cc in range(8)],
                    reads=["lnt4", "wvT"], writes=["SA2"])
                op("vector", (lambda t: lambda e: e.tensor_copy(
                    v_sb[:, t, :, 0:HD], SA2[:, 0:1024].rearrange("p (h d) -> p h d", h=H)))(tt),
                   reads=["SA2"], writes=["vbuf"])
            else:
                group("tensor", [
                    (lambda c, t_: lambda e: e.matmul(
                        gps[:, 0:512], lnt4[:, c, t_ * 128:(t_ + 1) * 128],
                        wvT(c, 0, 512),
                        start=(c == 0), stop=(c == 7), skip_group_check=True))(cc, t4)
                    for cc in range(8)], reads=["lnt4", "wvT"], writes=["gps"])
                group("tensor", [
                    (lambda c, t_: lambda e: e.matmul(
                        ups[:, 0:512], lnt4[:, c, t_ * 128:(t_ + 1) * 128],
                        wvT(c, 512, 1024),
                        start=(c == 0), stop=(c == 7), skip_group_check=True))(cc, t4)
                    for cc in range(8)], reads=["lnt4", "wvT"], writes=["ups"])
                op("vector", (lambda t: lambda e: e.tensor_copy(
                    v_sb[:, t, 0:8, 0:HD], gps[:, 0:512].rearrange("p (h d) -> p h d", h=8)))(tt),
                   reads=["gps"], writes=["vbuf"])
                op("vector", (lambda t: lambda e: e.tensor_copy(
                    v_sb[:, t, 8:16, 0:HD], ups[:, 0:512].rearrange("p (h d) -> p h d", h=8)))(tt),
                   reads=["ups"], writes=["vbuf"])
        sq_ap, sq_nm = (scrbf[:, 0:512], "scrbf") if vol % 2 == 0 else (sqtb[:], "sqtb")
        for (srcfn, srcnm, sp, bdnm, bdap, kind) in (
                (lambda a=vs, b=ve: q1x[0:64, a:b], "q1x", 64, "bd1", bd1_sb[:], "q1"),
                (lambda a=vs, b=ve: q2T[:, a:b], "q2T", 128, "bd2", bd2_sb[:], "q2"),
                (lambda a=vs, b=ve: k1T(a, b), "k1Tb", 64, "bd1", bd1_sb[:], "k1"),
                (lambda a=vs, b=ve: k2T(a, b), "k2Tb", 128, "bd2", bd2_sb[:], "k2")):
            op("scalar", (lambda s_, p_, q=sq_ap: lambda e: e.activation(q[0:p_, :], s_(), AF.Square))(srcfn, sp),
               reads=[srcnm], writes=[sq_nm])
            op("tensor", (lambda b_, q=sq_ap: lambda e: e.matmul(
                accb[0][0:16, 0:512], b_, q[:, :], start=True, stop=True,
                skip_group_check=True))(bdap),
               reads=[sq_nm, bdnm], writes=["acc0"])
            if kind in ("q1", "q2"):
                i = 0 if kind == "q1" else 1
                op("vector", (lambda ii, a, b: lambda e: e.tensor_copy(
                    nq(ii, a, b), accb[0][0:16, 0:512]))(i, vs, ve),
                   reads=["acc0"], writes=["nq%d" % i])
            elif kind == "k1":
                op("vector", (lambda a, b: lambda e: e.tensor_copy(
                    nkp(a, b), accb[0][0:16, 0:512]))(vs, ve),
                   reads=["acc0"], writes=["nkpb"])
            else:
                op("vector", (lambda a, b: lambda e: e.tensor_tensor(
                    nkp(a, b), nkp(a, b), accb[0][0:16, 0:512], AluOpType.mult))(vs, ve),
                   reads=["acc0", "nkpb"], writes=["nkpb"])
        op("vector", (lambda a, b, q=sq_ap: lambda e: e.tensor_tensor(
            q[0:64, :], q1x[0:64, a:b], k1T(a, b), AluOpType.mult))(vs, ve),
           reads=["q1x", "k1Tb"], writes=[sq_nm])
        op("tensor", (lambda q=sq_ap: lambda e: e.matmul(
            accb[0][0:16, 0:512], bd1_sb[:], q[:, :], start=True, stop=True,
            skip_group_check=True))(),
           reads=[sq_nm, "bd1"], writes=["acc0"])
        op("vector", (lambda a, b: lambda e: e.tensor_copy(dg(a, b), accb[0][0:16, 0:512]))(vs, ve),
           reads=["acc0"], writes=["dgb"])
        op("vector", (lambda a, b, q=sq_ap: lambda e: e.tensor_tensor(
            q[:, :], q2T[:, a:b], k2T(a, b), AluOpType.mult))(vs, ve),
           reads=["q2T", "k2Tb"], writes=[sq_nm])
        op("tensor", (lambda q=sq_ap: lambda e: e.matmul(
            accb[0][0:16, 0:512], bd2_sb[:], q[:, :], start=True, stop=True,
            skip_group_check=True))(),
           reads=[sq_nm, "bd2"], writes=["acc0"])
        op("vector", (lambda a, b: lambda e: e.tensor_tensor(
            dg(a, b), dg(a, b), accb[0][0:16, 0:512], AluOpType.mult))(vs, ve),
           reads=["acc0", "dgb"], writes=["dgb"])

    for g8 in range(8):
        for qc in range(4):
            a, b = qc * 512, (qc + 1) * 512
            op("tensor", (lambda gg, aa, bb: lambda e: e.matmul(
                gps[:, 0:512], rep1a_sb[:, gg, :], pbuf[0:80, PB_K1 + aa:PB_K1 + bb],
                start=True, stop=True, skip_group_check=True))(g8, a, b),
               reads=["rep1a", "k1Tb"], writes=["gps"])
            op("tensor", (lambda gg, aa, bb: lambda e: e.matmul(
                ups[:, 0:512], rep2_sb[:, gg, :], k2T(aa, bb),
                start=True, stop=True, skip_group_check=True))(g8, a, b),
               reads=["rep2", "k2Tb"], writes=["ups"])
            op("scalar", lambda e: e.activation(t1[:], gps[:, 0:512], AF.Copy),
               reads=["gps"], writes=["t1"])
            op("vector", (lambda gg, aa, bb: lambda e: e.tensor_tensor(
                kk_sb[0:32, gg, aa:bb], t1[0:32, :], ups[0:32, 0:512], AluOpType.mult))(g8, a, b),
               reads=["t1", "ups"], writes=["kk"])
            op("scalar", (lambda gg, aa, bb: lambda e: e.activation(
                kk_sb[32:33, gg, aa:bb], t1[32:33, :], AF.Copy))(g8, a, b),
               reads=["t1"], writes=["kk"])
            op("vector", (lambda gg, aa, bb: lambda e: e.tensor_tensor(
                kk_sb[64:96, gg, aa:bb], t1[64:96, :], ups[64:96, 0:512], AluOpType.mult))(g8, a, b),
               reads=["t1", "ups"], writes=["kk"])
            op("scalar", (lambda gg, aa, bb: lambda e: e.activation(
                kk_sb[96:97, gg, aa:bb], t1[96:97, :], AF.Copy))(g8, a, b),
               reads=["t1"], writes=["kk"])

    op("vector", lambda e: e.tensor_tensor(nq(0, 0, N), nq(0, 0, N), nq(1, 0, N),
                                           AluOpType.mult),
       reads=["nq0", "nq1"], writes=["nq0"])
    op("vector", lambda e: e.memset(nkp(0, 128), 0.0), reads=["nkpb"], writes=["nkpb"])
    scan_src, scan_dst = nkp, ping2
    sbufs = ["nkpb", "nq1"]
    step = 1
    while step < N:
        sfn, dfn = (nkp, ping2) if sbufs[0] == "nkpb" else (ping2, nkp)
        op("vector", (lambda s_, d_, st: lambda e: e.tensor_tensor(
            d_(st, N), s_(st, N), s_(0, N - st), AluOpType.max))(sfn, dfn, step),
           reads=[sbufs[0]], writes=[sbufs[1]])
        op("vector", (lambda s_, d_, st: lambda e: e.tensor_copy(d_(0, st), s_(0, st)))(sfn, dfn, step),
           reads=[sbufs[0]], writes=[sbufs[1]])
        sbufs = [sbufs[1], sbufs[0]]
        step *= 2
    NKP_FINAL = sbufs[0]
    nkp_fin = nkp if NKP_FINAL == "nkpb" else ping2


    group("tensor", [
        (lambda hf: lambda e: e.matmul(SA[:, hf * 512:(hf + 1) * 512], ones128[:],
                                       bo_bf[:, hf * 512:(hf + 1) * 512],
                                       start=True, stop=True, skip_group_check=True))(hf)
        for hf in range(2)],
        reads=["ones128", "bo_bf"], writes=["SA"])
    op("vector", lambda e: e.tensor_copy(bo_b[:], SA[:, 0:1024]), reads=["SA"], writes=["bo_b"])

    if dbg:
        tap(["q1x"], t_q1x, q1x[0:80, :])
        tap(["q2T"], t_q2T, q2T[:])
        tap(["dgb"], t_dg, dg(0, N))
        tap([NKP_FINAL], t_nkp, nkp_fin(0, N))
        tap(["kk"], t_kk, kk_sb[:].rearrange("p g n -> p (g n)"))
        tap(["nq0"], t_nq[0:16, 0:N], nq(0, 0, N))
        tap(["nq1"], t_nq[0:16, N:2 * N], nq(1, 0, N))
        tap(["vbuf"], t_v, v_sb[:].rearrange("p a b c -> p (a b c)"))

    op("gpsimd", lambda e: e.memset(pbuf[:, PB_WV:PB_WV + 16 * 512], 0.0),
       writes=["qqh"], extra_waits=list(G.B("wvT").write_waits().items()))
    for _s, _v in G.B("wk2w").write_waits().items():
        G.B("E1").r[_s] = max(G.B("E1").r.get(_s, 0), _v)
    for _nm in ("E0", "bo_b"):
        for _s, _v in G.B("k1Tb").write_waits().items():
            G.B(_nm).r[_s] = max(G.B(_nm).r.get(_s, 0), _v)
    for _nm in ("wgi0", "wgi1"):
        for _s, _v in G.B("lnt4").write_waits().items():
            G.B(_nm).r[_s] = max(G.B(_nm).r.get(_s, 0), _v)

    Dw0 = Dw[0:16, 0:512]
    Dw1 = Dw[0:16, 512:1024]
    SAb, SAn = [SA, SA2], ["SA", "SA2"]
    Ebb, Ebn = [Eb0, Eb1], ["E0", "E1"]
    ktpar = [0]
    hw_marks = {}
    for qb in range(NW):
        q0 = qb * WQ
        nkt = 4 * qb + 4
        def dpre_units(qb):
            q0 = qb * WQ

            def emit_build(g8):
                op("tensor", (lambda gg, q: lambda e: e.matmul(
                    SA[:, 0:512], rep1a_sb[:, gg, :], q1x[0:80, q:q + WQ],
                    start=True, stop=True, skip_group_check=True))(g8, q0),
                   reads=["rep1a", "q1x"], writes=["SA"])
                op("tensor", (lambda gg, q: lambda e: e.matmul(
                    SA[:, 512:1024], rep2_sb[:, gg, :], q2T[:, q:q + WQ],
                    start=True, stop=True, skip_group_check=True))(g8, q0),
                   reads=["rep2", "q2T"], writes=["SA"])
                op("vector", lambda e: e.tensor_copy(t1[:], SA[:, 0:512]),
                   reads=["SA"], writes=["t1"])
                op("vector", (lambda gg: lambda e: e.tensor_tensor(
                    qqh(2 * gg)[0:32, :], t1[0:32, :], SA[0:32, 512:1024], AluOpType.mult))(g8),
                   reads=["t1", "SA"], writes=["qqh"])
                op("vector", (lambda gg: lambda e: e.tensor_copy(
                    qqh(2 * gg)[32:33, :], t1[32:33, :]))(g8),
                   reads=["t1"], writes=["qqh"])
                op("vector", (lambda gg: lambda e: e.tensor_tensor(
                    qqh(2 * gg + 1)[64:96, :], t1[64:96, :], SA[64:96, 512:1024], AluOpType.mult))(g8),
                   reads=["t1", "SA"], writes=["qqh"])
                op("vector", (lambda gg: lambda e: e.tensor_copy(
                    qqh(2 * gg + 1)[96:97, :], t1[96:97, :]))(g8),
                   reads=["t1"], writes=["qqh"])

            def emit_minis(g8):
                for j2 in range(2):
                    hh = 2 * g8 + j2
                    co = 512 * (hh % 2)
                    fns = []
                    for qt in range(4):
                        masked = (qb == 0 and qt == 0)
                        fns.append((lambda hx=hh, t=qt, c=co, m=masked: lambda e: e.matmul(
                            SA2[:, c + t * 128:c + (t + 1) * 128],
                            qqh(hx)[:, t * 128:(t + 1) * 128], kk_sb[:, hx // 2, 0:128],
                            start=True, stop=not m, skip_group_check=True))())
                        if masked:
                            fns.append((lambda c=co: lambda e: e.matmul(
                                SA2[:, c:c + 128], identb_sb[:], maskqm_sb[:],
                                start=False, stop=True, skip_group_check=True))())
                    group("tensor", fns, reads=["qqh", "kk", "identb", "maskqm"], writes=["SA2"])
                    for qt in range(4):
                        op("vector", (lambda hx=hh, t=qt, c=co: lambda e: e.reduce_max(
                            Lmq[:, t, hx:hx + 1], SA2[:, c + t * 128:c + (t + 1) * 128],
                            axis=mybir.AxisListType.X))(),
                           reads=["SA2"], writes=["Lmq"])

            units = []
            for g8 in range(8):
                units.append((lambda gg=g8: emit_build(gg)))
                if g8 > 0:
                    units.append((lambda gg=g8 - 1: emit_minis(gg)))
            units.append(lambda: emit_minis(7))

            def emit_dchain():
                for qt in range(4):
                    op("tensor", (lambda t: lambda e: e.transpose(
                        SA[0:16, t * 128:(t + 1) * 128], Lmq[:, t, :], ident_sb[:]))(qt),
                       reads=["Lmq", "ident"], writes=["SA"])
                op("vector", lambda e: e.tensor_copy(Dw0, SA[0:16, 0:512]),
                   reads=["SA"], writes=["Dwb"])
                op("vector", (lambda q: lambda e: e.tensor_tensor(
                    ntmp, nq(0, q, q + WQ), nkp_fin(q, q + WQ), AluOpType.mult))(q0),
                   reads=["nq0", NKP_FINAL], writes=["scr32"])
                op("scalar", lambda e: e.activation(ntmp, ntmp, AF.Ln, bias=tinyb[0:16]),
                   reads=["scr32", "statc"], writes=["scr32"])
                op("scalar", lambda e: e.activation(ntmp, ntmp, AF.Exp, scale=0.5),
                   reads=["scr32"], writes=["scr32"])
                op("vector", lambda e: e.tensor_tensor(Dw1, ntmp, Dw0, AluOpType.max),
                   reads=["scr32", "Dwb"], writes=["Dwb"])
                op("vector", (lambda q: lambda e: e.tensor_tensor(
                    Dw0, Dw0, dg(q, q + WQ), AluOpType.max))(q0),
                   reads=["Dwb", "dgb"], writes=["Dwb"])
                op("vector", lambda e: e.tensor_scalar(Dw0, Dw0, float(C2), None, AluOpType.add),
                   reads=["Dwb"], writes=["Dwb"])
                op("vector", lambda e: e.tensor_tensor(Dw0, Dw0, Dw1, AluOpType.min),
                   reads=["Dwb"], writes=["Dwb"])
                op("vector", lambda e: e.tensor_scalar(Dw1, Dw1, float(C1), None, AluOpType.subtract),
                   reads=["Dwb"], writes=["Dwb"])
                op("vector", lambda e: e.tensor_tensor(Dw0, Dw0, Dw1, AluOpType.max),
                   reads=["Dwb"], writes=["Dwb"])
                op("vector", lambda e: e.tensor_scalar(Dw0, Dw0, float(DFLOOR), None, AluOpType.max),
                   reads=["Dwb"], writes=["Dwb"])
                op("vector", lambda e: e.tensor_copy(Dfull[0:16, :], Dw0),
                   reads=["Dwb"], writes=["Dfull"])
            units.append(emit_dchain)

            def emit_augfix(g8):
                op("tensor", (lambda gg: lambda e: e.matmul(
                    SA[:, 0:512], repadp_sb[:, gg, :], Dfull[:, :],
                    start=True, stop=True, skip_group_check=True))(g8),
                   reads=["repadp", "Dfull"], writes=["SA"])
                op("vector", (lambda gg: lambda e: e.tensor_copy(
                    qqh(2 * gg)[32:33, :], SA[32:33, 0:512]))(g8),
                   reads=["SA"], writes=["qqh"])
                op("vector", (lambda gg: lambda e: e.tensor_copy(
                    qqh(2 * gg + 1)[96:97, :], SA[96:97, 0:512]))(g8),
                   reads=["SA"], writes=["qqh"])
            for g8 in range(8):
                units.append((lambda gg=g8: emit_augfix(gg)))
            return units

        if qb == 0:
            for u in dpre_units(0):
                u()
        if dbg and qb == 0:
            tap(["qqh"], t_qq, pbuf[:, PB_WV:PB_WV + 8 * WQ])
        deferred = [None]

        def flush_deferred():
            if deferred[0] is not None:
                for f in deferred[0]:
                    f()
                deferred[0] = None

        for h in range(H):
            a = accb[h % 2]
            an = "acc%d" % (h % 2)
            npairs = (nkt + 1) // 2

            def emit_S(ktp, h=h, a=a, an=an):
                kts = [kt for kt in (2 * ktp, 2 * ktp + 1) if kt < nkt]
                pp = (ktpar[0] + ktp) % 2
                SAc, SAnm = SAb[pp], SAn[pp]
                fns = []
                rd = ["kk", "qqh"]
                for i, kt in enumerate(kts):
                    d = kt - 4 * qb
                    qlo = 128 * max(0, d)
                    diag = d >= 0
                    fns.append((lambda hx=h, k=kt, ql=qlo, ii=i, p=SAc, dgm=diag:
                                lambda e: e.matmul(
                                    p[:, ii * 512 + ql: ii * 512 + 512],
                                    kk_sb[:, hx // 2, k * 128:(k + 1) * 128],
                                    qqh(hx)[:, ql:WQ],
                                    start=True, stop=not dgm, skip_group_check=True))())
                    if diag:
                        fns.append((lambda ql=qlo, ii=i, p=SAc: lambda e: e.matmul(
                            p[:, ii * 512 + ql: ii * 512 + ql + 128],
                            identb_sb[:], mask2_sb[:],
                            start=False, stop=True, skip_group_check=True))())
                        rd = ["kk", "qqh", "identb", "mask2"]
                group("tensor", fns, reads=rd, writes=[SAnm])

            def emit_exp(ktp, h=h):
                kts = [kt for kt in (2 * ktp, 2 * ktp + 1) if kt < nkt]
                pp = (ktpar[0] + ktp) % 2
                SAc, SAnm = SAb[pp], SAn[pp]
                Ebc, Ebnm = Ebb[pp], Ebn[pp]
                wid = 512 * len(kts)
                op("scalar", (lambda w, p, eb: lambda e: e.activation(
                    eb[:, 0:w], p[:, 0:w], AF.Exp, scale=SCALE))(wid, SAc, Ebc),
                   reads=[SAnm], writes=[Ebnm])
                if dbg and qb == 0 and h == 0 and ktp == 0:
                    tap([Ebnm], t_E, Ebc[:])

            def emit_V(ktp, h=h, a=a, an=an):
                kts = [kt for kt in (2 * ktp, 2 * ktp + 1) if kt < nkt]
                pp = (ktpar[0] + ktp) % 2
                Ebc, Ebnm = Ebb[pp], Ebn[pp]
                for i, kt in enumerate(kts):
                    qlo = 128 * max(0, kt - 4 * qb)
                    mk = op("tensor", (lambda k, ii, ql, hx, aa, nk, eb: lambda e: e.matmul(
                        aa[0:HD + 1, ql:WQ], v_sb[:, k, hx, :],
                        eb[:, ii * 512 + ql: ii * 512 + 512],
                        start=(k == 0), stop=(k == nk - 1),
                        skip_group_check=True))(kt, i, qlo, h, a, nkt, Ebc),
                        reads=[Ebnm, "vbuf", "vones"],
                        writes=([an] if kt == 0 else []))
                    if kt == nkt - 1:
                        G.B(an).did_write(mk)

            emit_S(0)
            fl = 0
            for t in range(npairs):
                if t + 1 < npairs:
                    emit_S(t + 1)
                if t == fl:
                    flush_deferred()
                emit_exp(t)
                emit_V(t)
            ktpar[0] += npairs
            if h % 2 == 0:
                op("vector", lambda e: e.tensor_copy(Dw[0:1, 0:512], accb[0][HD:HD + 1, :]),
                   reads=["acc0"], writes=["Dwb"])
            else:
                op("vector", lambda e: e.tensor_copy(Dw[0:1, 512:1024], accb[1][HD:HD + 1, :]),
                   reads=["acc1"], writes=["Dwb"])
                op("scalar", lambda e: e.activation(Dw[0:1, 0:1024], Dw[0:1, 0:1024],
                                                    AF.Ln, bias=tinyb[0:1]),
                   reads=["Dwb", "statc"], writes=["Dwb"])
                op("scalar", lambda e: e.activation(rrpad[0:1, 0:1024], Dw[0:1, 0:1024],
                                                    AF.Exp, scale=-1.0),
                   reads=["Dwb"], writes=["scrbf_rr"])

                def fin_pair(h=h):
                    for hx in (h - 1, h):
                        co = 512 * (hx % 2)
                        aa, an2 = accb[hx % 2], "acc%d" % (hx % 2)
                        op("tensor", (lambda c=co: lambda e: e.matmul(
                            gps[0:64, 0:512], onesel[:], rrpad[:, c:c + 512],
                            start=True, stop=True, skip_group_check=True))(),
                           reads=["onesel", "scrbf_rr"], writes=["gps"])
                        op("vector", lambda e: e.tensor_copy(rrs, gps[0:64, 0:512]),
                           reads=["gps"], writes=["scr32"])
                        if dbg and qb == 0 and hx == 0:
                            tap(["scr32"], t_rr, rrs)
                        op("vector", (lambda hy=hx, ab=aa: lambda e: e.tensor_tensor(
                            aoT[64 * (hy % 2):64 * (hy % 2) + 64, hy // 2, :],
                            ab[0:HD, :], rrs, AluOpType.mult))(),
                           reads=[an2, "scr32"], writes=["aoT"])
                deferred[0] = [fin_pair]
        flush_deferred()

        for tt in range(4):
            trow = q0 + tt * 128
            wps, wpn = (SA, "SA") if tt % 2 == 0 else (SA2, "SA2")
            fns = []
            for cc in range(8):
                for half in range(2):
                    def fo(c=cc, hf=half, t=tt, p=wps):
                        def f(e):
                            return e.matmul(p[:, hf * 512:(hf + 1) * 512],
                                            aoT[:, c, t * 128:(t + 1) * 128],
                                            wot_sb[:, c, hf * 512:(hf + 1) * 512],
                                            start=(c == 0), stop=(c == 7), skip_group_check=True)
                        return f
                    fns.append(fo())
            group("tensor", fns, reads=["wot", "aoT"], writes=[wpn])
            pi = tt % 2
            xn, ln_ = "xp%d" % pi, "lnp%d" % pi
            op("sync", (lambda r, p: lambda e: e.dma_start(out=xp[p][:], in_=x_d[r:r + 128, :]))(trow, pi),
               writes=[xn], dma_sem=sSx[pi])
            for ch in range(2):
                op("vector", (lambda p, cch, wq=wps: lambda e: e.scalar_tensor_tensor(
                    scr32[:], wq[:, cch * 512:(cch + 1) * 512], 1.0,
                    xp[p][:, cch * 512:(cch + 1) * 512],
                    AluOpType.mult, AluOpType.add))(pi, ch),
                   reads=[wpn, xn], writes=["scr32"])
                op("vector", (lambda p, cch: lambda e: e.tensor_tensor(
                    xp[p][:, cch * 512:(cch + 1) * 512], scr32[:],
                    bo_b[:, cch * 512:(cch + 1) * 512], AluOpType.add))(pi, ch),
                   reads=["scr32", "bo_b"], writes=[xn])
            op("sync", (lambda r, p: lambda e: e.dma_start(out=y_d[r:r + 128, :], in_=xp[p][:]))(trow, pi),
               reads=[xn], writes=["yb_%d_%d" % (qb, tt)], dma_sem=sSy)
            ln_full(pi, xn, xp[pi][:], ln_, lnp[pi][:])
            op("sync", (lambda t, p: lambda e: e.dma_start_transpose(
                ln2T[:, :, t * 128:(t + 1) * 128], lnp[p][:]))(tt, pi),
               reads=[ln_], writes=["ln2Tbuf"] if tt == 3 else [], dma_sem=sT2)
            if tt < 3:
                G.B("ln2Tbuf").did_write((sT2, sT2.v))
        if dbg and qb == 0:
            tap(["aoT"], t_ao, aoT[:].rearrange("p a b -> p (a b)"))
            tap(["ln2Tbuf"], t_ln2, ln2T[:].rearrange("p a b -> p (a b)"))

        G.ops["scalar"].append((load_table(SILU_ID), [], None))
        if qb + 1 < NW:
            _units = dpre_units(qb + 1)
            nbm = 2 * 8
            u_bm, u_dchain, u_aug = _units[0:nbm], _units[nbm], _units[nbm + 1:]
            assert len(u_aug) == 8
        else:
            u_bm, u_dchain, u_aug = [], None, []

        def ld_wgu(c):
            b = c % 3
            b2 = c % 2
            op("sync", (lambda cx, bx: lambda e: e.dma_start(
                out=wgi[bx][:], in_=wg_d[:, cx * 128:(cx + 1) * 128].rearrange("(cc p) f -> p cc f", p=128)))(c, b2),
               writes=["wgi%d" % b2], dma_sem=sWg)
            op("vector", (lambda bx, bi: lambda e: e.tensor_copy(
                wgb[bx][:], wgi[bi][:]))(b, b2),
               reads=["wgi%d" % b2], writes=["wgb%d" % b])
            op("gpsimd", (lambda cx, bx: lambda e: e.dma_start(
                out=wub[bx][:], in_=wu_d[:, cx * 128:(cx + 1) * 128].rearrange("(cc p) f -> p cc f", p=128)))(c, b),
               writes=["wub%d" % b], dma_sem=sWu)
        ld_wgu(0)
        ld_wgu(1)
        for hc in range(NHC):
            b = hc % 3
            if hc + 2 < NHC:
                ld_wgu(hc + 2)
            if 1 <= hc < 10:
                while u_bm and len(u_bm) > 16 * (9 - hc) // 9:
                    u_bm.pop(0)()
            if hc == 10 and u_dchain is not None:
                G.ops["scalar"].append((load_table(NLX_ID), [], None))
                u_dchain()
                G.ops["scalar"].append((load_table(SILU_ID), [], None))
            if hc >= 11:
                while u_aug and len(u_aug) > 8 * (NHC - 1 - hc) // 13:
                    u_aug.pop(0)()
            group("tensor", [
                (lambda c, bx: lambda e: e.matmul(gps[:, 0:512], wgb[bx][:, c, :], ln2T[:, c, :],
                                                  start=(c == 0), stop=(c == 7), skip_group_check=True))(cc, b)
                for cc in range(8)], reads=["wgb%d" % b, "ln2Tbuf"], writes=["gps"])
            group("tensor", [
                (lambda c, bx: lambda e: e.matmul(ups[:, 0:512], wub[bx][:, c, :], ln2T[:, c, :],
                                                  start=(c == 0), stop=(c == 7), skip_group_check=True))(cc, b)
                for cc in range(8)], reads=["wub%d" % b, "ln2Tbuf"], writes=["ups"])
            op("scalar", lambda e: e.activation(t1[:], gps[:, 0:512], AF.Silu, scale=QS),
               reads=["gps"], writes=["t1"])
            op("vector", lambda e: e.tensor_tensor(hch, t1[:], ups[:, 0:512], AluOpType.mult),
               reads=["t1", "ups"], writes=["scrbf"])
            if dbg and qb == 0 and hc == 0:
                tap(["scrbf"], t_h, hch)
            op("gpsimd", (lambda c, q: lambda e: e.dma_start(
                out=hts_d[c * 128:(c + 1) * 128, q:q + WQ], in_=hch))(hc, q0),
               reads=["scrbf"], writes=["htsbuf"] if hc == NHC - 1 else [], dma_sem=sHw[qb])
            if hc < NHC - 1:
                G.B("htsbuf").did_write((sHw[qb], sHw[qb].v))
        G.ops["scalar"].append((load_table(NLX_ID), [], None))
        hw_marks[qb] = (sHw[qb], sHw[qb].v)

    wdw = {}
    for nm in ("qqh", "k1Tb", "k2Tb", "nq0", "nq1", "wq1w", "wq2w", "wk1w",
               "E0", "bo_b", "lnsq0", "lnsq1", "sqtb", "dgb", "nkpb"):
        for s, v in G.B(nm).write_waits().items():
            wdw[s] = max(wdw.get(s, 0), v)
    op("gpsimd", lambda e: e.dma_start(
        out=pbuf[:, 0:NHC * C].rearrange("p (hc f) -> p hc f", hc=NHC),
        in_=wd_d.rearrange("(hc p) f -> p hc f", p=128)),
       writes=["wdT"], dma_sem=sGw, extra_waits=list(wdw.items()))

    hTt2 = ln2T[:].rearrange("p a b -> p (a b)")[:, 0:NHC * 128].rearrange(
        "p (hc t) -> p hc t", hc=NHC)
    for tq in range(NTT):
        trow = tq * 128
        htv = [hTt, hTt2][tq % 2]
        htn = ["aoT", "ln2Tbuf"][tq % 2]
        op("gpsimd", (lambda r, hv: lambda e: e.dma_start(
            out=hv, in_=hts_d[:, r:r + 128].rearrange("(hc p) t -> p hc t", p=128)))(trow, htv),
           reads=["htsbuf"], writes=[htn], dma_sem=sHl,
           extra_waits=[hw_marks[tq // 4]])
        pi = tq % 2
        xn = "xp%d" % pi
        op("sync", (lambda r, p: lambda e: e.dma_start(out=xp[p][:], in_=y_d[r:r + 128, :]))(trow, pi),
           writes=[xn], dma_sem=sSx[pi],
           extra_waits=[(sSy, sSy.v)])
        if dbg and tq == 0:
            tap([xn], t_x2r, xp[pi][:])
            tap(["aoT"], t_hT, hTt.rearrange("p a b -> p (a b)"))
        for ch in range(2):
            group("tensor", [
                (lambda c, cch, hv: lambda e: e.matmul(
                    SA2[:, 0:512], hv[:, c, :], wdT(c, cch * 512, (cch + 1) * 512),
                    start=(c == 0), stop=(c == NHC - 1), skip_group_check=True))(hc, ch, htv)
                for hc in range(NHC)], reads=[htn, "wdT"], writes=["SA2"])
            op("vector", (lambda cch, p: lambda e: e.scalar_tensor_tensor(
                scr32[:], SA2[:, 0:512], QS2,
                xp[p][:, cch * 512:(cch + 1) * 512],
                AluOpType.mult, AluOpType.add))(ch, pi),
               reads=["SA2", xn], writes=["scr32"])
            if dbg and tq == 0:
                tap(["scr32"], t_f0 if ch == 0 else t_f1, scr32[:])
            op("sync", (lambda r, cch: lambda e: e.dma_start(
                out=y_d[r:r + 128, cch * 512:(cch + 1) * 512], in_=scr32[:]))(trow, ch),
               reads=["scr32"], writes=["yf_%d_%d" % (tq, ch)], dma_sem=sSy)

    G.wait_only("sync", {sSy: sSy.v})
    G.wait_only("gpsimd", {sHl: sHl.v, sGw: sGw.v, sTap: sTap.v})

    with nc.Block() as block:
        def emitter(engname):
            def emit(eng):
                last = {}
                for fn, waits, inc in G.ops[engname]:
                    for s, v in waits:
                        if last.get(s.name, 0) < v:
                            eng.wait_ge(s.h, v)
                            last[s.name] = v
                    ins = fn(eng) if fn is not None else None
                    if inc is not None and ins is not None:
                        s, amt = inc
                        ins.then_inc(s.h, amt)
            return emit
        block.tensor(emitter("tensor"))
        block.scalar(emitter("scalar"))
        block.vector(emitter("vector"))
        block.gpsimd(emitter("gpsimd"))
        block.sync(emitter("sync"))

    es.close()
    return nc


_NC_CACHE = {}


def _get_nc():
    if "nc" not in _NC_CACHE:
        _NC_CACHE["nc"] = build_nc()
    return _NC_CACHE["nc"]


def make_consts():
    ident = np.eye(128, dtype=np.float32)
    mask2 = (-1e9 * np.tril(np.ones((128, 128), np.float32), -1)).astype(np.float32)
    maskqm = (-1e9 * np.triu(np.ones((128, 128), np.float32), 1)).astype(np.float32)
    rep1a = np.zeros((80, 8, 128), np.float32)
    rep2 = np.zeros((128, 8, 128), np.float32)
    repadp = np.zeros((128, 8, 128), np.float32)
    for g8 in range(8):
        for j2 in range(2):
            hh = 2 * g8 + j2
            for r in range(R1):
                for s in range(R2):
                    rep1a[R1 * hh + r, g8, 64 * j2 + R2 * r + s] = 1.0
                    rep2[R2 * hh + s, g8, 64 * j2 + R2 * r + s] = 1.0
            rep1a[64 + hh, g8, 64 * j2 + 32] = 1.0
        repadp[2 * g8, g8, 32] = 1.0
        repadp[2 * g8 + 1, g8, 96] = 1.0
    bd1 = np.zeros((128, 16), np.float32)
    bd2 = np.zeros((128, 16), np.float32)
    for hh in range(16):
        bd1[R1 * hh:R1 * hh + R1, hh] = 1.0
        bd2[R2 * hh:R2 * hh + R2, hh] = 1.0
    return dict(ident=ident, identb=ident.copy(), mask2=mask2, maskqm=maskqm,
                rep1a=rep1a, rep2=rep2, repadp=repadp, bd1=bd1, bd2=bd2)


def kernel(L0_attn_x, L0_attn_Wq1, L0_attn_Wk1, L0_attn_Wq2, L0_attn_Wk2,
           L0_attn_Wv, L0_attn_Wo, L0_attn_bo, L0_ffn_w_gate, L0_ffn_w_up,
           L0_ffn_w_down):
    x = np.ascontiguousarray(np.asarray(L0_attn_x, dtype=np.float32))
    B = x.shape[0]
    shared = dict(
        wq1t=np.ascontiguousarray(np.asarray(L0_attn_Wq1, np.float32).T),
        wk1t=np.ascontiguousarray(np.asarray(L0_attn_Wk1, np.float32).T),
        wq2t=np.ascontiguousarray(np.asarray(L0_attn_Wq2, np.float32).T),
        wk2t=np.ascontiguousarray(np.asarray(L0_attn_Wk2, np.float32).T),
        wvt=np.ascontiguousarray(np.asarray(L0_attn_Wv, np.float32).T),
        wot=np.ascontiguousarray(np.asarray(L0_attn_Wo, np.float32).T),
        bo=np.ascontiguousarray(np.asarray(L0_attn_bo, np.float32).reshape(1, C)),
        wgt=np.ascontiguousarray(np.asarray(L0_ffn_w_gate, np.int32).T),
        wut=np.ascontiguousarray(np.asarray(L0_ffn_w_up, np.int32).T),
        wdt=np.ascontiguousarray(np.asarray(L0_ffn_w_down, np.int32).T),
        **make_consts())
    nc = _get_nc()
    in_maps = [dict(x=np.ascontiguousarray(x[b]), **shared) for b in range(B)]
    res = run_bass_kernel_spmd(nc, in_maps, core_ids=list(range(B)))
    out = np.stack([res.results[b]["y"] for b in range(B)], axis=0)
    return out.astype(np.float32)


# version 26
# speedup vs baseline: 1.1309x; 1.0416x over previous
import contextlib
import numpy as np
import concourse.bass as bass
import concourse.mybir as mybir
from concourse.bass_utils import run_bass_kernel_spmd
from concourse.alu_op_type import AluOpType

F32 = mybir.dt.float32
BF16 = mybir.dt.bfloat16
I32 = mybir.dt.int32
AF = mybir.ActivationFunctionType

N = 2048
C = 1024
H = 16
HD = 64
R1, R2 = 4, 8
SCALE = float(32 ** -0.5)
C1 = float(np.float32(72.0 / SCALE))
C2 = float(np.float32(45.0 / SCALE))
TWIN = float(np.float32(C1 + C2))
DFLOOR = -400.0
QS = float(np.float32(1.0 / 127.0))
QS2 = float(np.float32(QS) * np.float32(QS))
HID = 3072
NHC = HID // 128
NW = 4
WQ = 512
NTT = N // 128
EPS = 1e-5


class Sem:
    def __init__(self, nc, es, name):
        self.h = es.enter_context(nc.semaphore(name))
        self.v = 0
        self.name = name


class Buf:
    def __init__(self):
        self.w = {}
        self.r = {}

    def read_waits(self):
        return dict(self.w)

    def write_waits(self):
        m = dict(self.w)
        for s, v in self.r.items():
            m[s] = max(m.get(s, 0), v)
        return m

    def did_read(self, mk):
        if mk:
            s, v = mk
            self.r[s] = max(self.r.get(s, 0), v)

    def did_write(self, mk):
        self.w = {}
        self.r = {}
        if mk:
            s, v = mk
            self.w[s] = v


class Gen:
    ENGS = ("tensor", "scalar", "vector", "gpsimd", "sync")

    def __init__(self, nc, es):
        self.nc = nc
        self.ops = {e: [] for e in self.ENGS}
        self.bufs = {}
        self.eng_sem = {e: Sem(nc, es, "sem_" + e) for e in self.ENGS}

    def B(self, name):
        if name not in self.bufs:
            self.bufs[name] = Buf()
        return self.bufs[name]

    def group(self, eng, fns, reads=(), writes=(), extra_waits=(), dma_sem=None):
        waits = {}
        for nm in reads:
            for s, v in self.B(nm).read_waits().items():
                waits[s] = max(waits.get(s, 0), v)
        for nm in writes:
            for s, v in self.B(nm).write_waits().items():
                waits[s] = max(waits.get(s, 0), v)
        for s, v in extra_waits:
            waits[s] = max(waits.get(s, 0), v)
        for s, v in waits.items():
            assert v <= s.v, f"wait {s.name}>={v} but scheduled only {s.v}"
        sem = dma_sem if dma_sem is not None else self.eng_sem[eng]
        amt = 16 if dma_sem is not None else 1
        mk = None
        n = len(fns)
        for i, fn in enumerate(fns):
            w = list(waits.items()) if i == 0 else []
            inc = None
            if i == n - 1:
                sem.v += amt
                inc = (sem, amt)
                mk = (sem, sem.v)
            self.ops[eng].append((fn, w, inc))
        for nm in reads:
            self.B(nm).did_read(mk)
        for nm in writes:
            self.B(nm).did_write(mk)
        return mk

    def op(self, eng, fn, **kw):
        return self.group(eng, [fn], **kw)

    def wait_only(self, eng, waits):
        self.ops[eng].append((None, list(waits.items()), None))


def build_nc(dbg=False):
    nc = bass.Bass("TRN2", target_bir_lowering=False, debug=False)
    es = contextlib.ExitStack()

    x_d = nc.dram_tensor("x", [N, C], F32, kind="ExternalInput").ap()
    y_d = nc.dram_tensor("y", [N, C], F32, kind="ExternalOutput").ap()
    wq1_d = nc.dram_tensor("wq1t", [C, H * R1], F32, kind="ExternalInput").ap()
    wq2_d = nc.dram_tensor("wq2t", [C, H * R2], F32, kind="ExternalInput").ap()
    wk1_d = nc.dram_tensor("wk1t", [C, H * R1], F32, kind="ExternalInput").ap()
    wk2_d = nc.dram_tensor("wk2t", [C, H * R2], F32, kind="ExternalInput").ap()
    wv_d = nc.dram_tensor("wvt", [C, C], F32, kind="ExternalInput").ap()
    wo_d = nc.dram_tensor("wot", [C, C], F32, kind="ExternalInput").ap()
    bo_d = nc.dram_tensor("bo", [1, C], F32, kind="ExternalInput").ap()
    wg_d = nc.dram_tensor("wgt", [C, HID], I32, kind="ExternalInput").ap()
    wu_d = nc.dram_tensor("wut", [C, HID], I32, kind="ExternalInput").ap()
    wd_d = nc.dram_tensor("wdt", [HID, C], I32, kind="ExternalInput").ap()
    ident_d = nc.dram_tensor("ident", [128, 128], F32, kind="ExternalInput").ap()
    identb_d = nc.dram_tensor("identb", [128, 128], F32, kind="ExternalInput").ap()
    mask2_d = nc.dram_tensor("mask2", [128, 128], F32, kind="ExternalInput").ap()
    maskqm_d = nc.dram_tensor("maskqm", [128, 128], F32, kind="ExternalInput").ap()
    rep1a_d = nc.dram_tensor("rep1a", [128, 8, 128], F32, kind="ExternalInput").ap()
    rep2_d = nc.dram_tensor("rep2", [128, 8, 128], F32, kind="ExternalInput").ap()
    repadp_d = nc.dram_tensor("repadp", [128, 8, 128], F32, kind="ExternalInput").ap()
    bd1_d = nc.dram_tensor("bd1", [128, 16], F32, kind="ExternalInput").ap()
    bd2_d = nc.dram_tensor("bd2", [128, 16], F32, kind="ExternalInput").ap()
    hts_d = nc.dram_tensor("hts", [HID, N], BF16, kind="Internal").ap()
    if dbg:
        t_q1x = nc.dram_tensor("t_q1x", [80, N], F32, kind="ExternalOutput").ap()
        t_q2T = nc.dram_tensor("t_q2T", [128, N], F32, kind="ExternalOutput").ap()
        t_dg = nc.dram_tensor("t_dg", [16, N], F32, kind="ExternalOutput").ap()
        t_nkp = nc.dram_tensor("t_nkp", [16, N], F32, kind="ExternalOutput").ap()
        t_kk = nc.dram_tensor("t_kk", [128, 8 * N], F32, kind="ExternalOutput").ap()
        t_nq = nc.dram_tensor("t_nq", [16, 2 * N], F32, kind="ExternalOutput").ap()
        t_v = nc.dram_tensor("t_v", [128, NTT * H * (HD + 1)], F32, kind="ExternalOutput").ap()
        t_qq = nc.dram_tensor("t_qq", [128, 8 * WQ], F32, kind="ExternalOutput").ap()
        t_E = nc.dram_tensor("t_E", [128, 1024], F32, kind="ExternalOutput").ap()
        t_rr = nc.dram_tensor("t_rr", [64, WQ], F32, kind="ExternalOutput").ap()
        t_ao = nc.dram_tensor("t_ao", [128, 8 * WQ], F32, kind="ExternalOutput").ap()
        t_ln2 = nc.dram_tensor("t_ln2", [128, 8 * WQ], F32, kind="ExternalOutput").ap()
        t_h = nc.dram_tensor("t_h", [128, WQ], F32, kind="ExternalOutput").ap()
        t_x2r = nc.dram_tensor("t_x2r", [128, C], F32, kind="ExternalOutput").ap()
        t_hT = nc.dram_tensor("t_hT", [128, NHC * 128], F32, kind="ExternalOutput").ap()
        t_f0 = nc.dram_tensor("t_f0", [128, WQ], F32, kind="ExternalOutput").ap()
        t_f1 = nc.dram_tensor("t_f1", [128, WQ], F32, kind="ExternalOutput").ap()

    A = lambda name, shape, dt: nc.alloc_sbuf_tensor(name, shape, dt)
    v_sb = A("v_sb", [128, NTT, H, HD + 1], BF16)
    kk_sb = A("kk_sb", [128, 8, N], BF16)
    q1x = A("q1x", [128, N], BF16)
    q2T = A("q2T", [128, N], BF16)
    ident_sb = A("ident_sb", [128, 128], F32)
    identb_sb = A("identb_sb", [128, 128], BF16)
    mask2_sb = A("mask2_sb", [128, 128], BF16)
    maskqm_sb = A("maskqm_sb", [128, 128], BF16)
    Lmq = A("Lmq", [128, 4, 16], F32)
    Dw = A("Dw", [16, 1024], F32)
    Dfull = A("Dfull", [128, 512], BF16)
    rrpad = A("rrpad", [128, 1024], BF16)
    onesel = A("onesel", [128, 64], BF16)
    rep1a_sb = A("rep1a_sb", [128, 8, 128], BF16)
    rep2_sb = A("rep2_sb", [128, 8, 128], BF16)
    repadp_sb = A("repadp_sb", [128, 8, 128], BF16)
    bd1_sb = A("bd1_sb", [128, 16], BF16)
    bd2_sb = A("bd2_sb", [128, 16], BF16)
    ones128 = A("ones128", [1, 128], BF16)
    bo_bf = A("bo_bf", [1, C], BF16)
    xp = [A("xp0", [128, C], F32), A("xp1", [128, C], F32)]
    lnp = [A("lnp0", [128, C], BF16), A("lnp1", [128, C], BF16)]
    statsv = [A("stats%d" % i, [128, 8], F32) for i in range(4)]
    statc = A("statc", [128, 3], F32)
    epsb = statc[:, 0:1]
    tinyb = statc[:, 1:2]
    sh20 = statc[:, 2:3]
    scrbf = A("scrbf", [128, WQ], BF16)
    sqt = scrbf[:, 384:512]
    lnt4_off = (nc.sbuf_base + 31) // 32 * 32
    lnt4 = A("lnt4", [128, 8, 512], BF16)
    wgi = [nc.alloc_sbuf_tensor_at("wgi%d" % i, [128, 8, 128], I32,
                                   offset=lnt4_off + i * 4096) for i in range(2)]
    rrb = rrpad[0:1, 0:WQ]
    hch = scrbf[:, 0:WQ]
    scr32 = A("scr32", [128, WQ], F32)
    lnr = scr32[0:1, :]
    rrs = scr32[64:128, :]
    ntmp = scr32[0:16, :]
    t1 = A("t1", [128, WQ], BF16)
    aoT = A("aoT", [128, 8, WQ], BF16)
    hTt = aoT[:].rearrange("p a b -> p (a b)")[:, 0:NHC * 128].rearrange(
        "p (hc t) -> p hc t", hc=NHC)
    ln2T = A("ln2T", [128, 8, WQ], BF16)
    wgb = [A("wgb%d" % i, [128, 8, 128], BF16) for i in range(3)]
    wub = [A("wub%d" % i, [128, 8, 128], BF16) for i in range(3)]
    wot_sb = A("wot_sb", [128, 8, C], BF16)
    pbuf = A("pbuf", [128, 25600], BF16)

    PB_WV = 0
    PB_K1 = 8192
    PB_K2 = 10240
    PB_NQ1 = 12288
    PB_NQ2 = 14336
    PB_DG = 16384
    PB_NKP = 18432
    PB_LT0 = 20480
    PB_LT1 = 21504
    PB_WQ1 = 22528
    PB_WQ2 = 23040
    PB_WK1 = 24064
    PB_WK2 = 24576

    def wvT(cc, f0, f1):
        return pbuf[:, PB_WV + cc * C + f0: PB_WV + cc * C + f1]

    def k1T(t0, t1):
        return pbuf[0:64, PB_K1 + t0: PB_K1 + t1]

    def k2T(t0, t1):
        return pbuf[0:128, PB_K2 + t0: PB_K2 + t1]

    def nq(i, t0, t1):
        off = PB_NQ1 if i == 0 else PB_NQ2
        return pbuf[0:16, off + t0: off + t1]

    def dg(t0, t1):
        return pbuf[0:16, PB_DG + t0: PB_DG + t1]

    def nkp(t0, t1):
        return pbuf[0:16, PB_NKP + t0: PB_NKP + t1]

    def ping2(t0, t1):
        return pbuf[0:16, PB_NQ2 + t0: PB_NQ2 + t1]

    def wq1w(cc):
        return pbuf[:, PB_WQ1 + cc * 64: PB_WQ1 + (cc + 1) * 64]

    def wq2w(cc):
        return pbuf[:, PB_WQ2 + cc * 128: PB_WQ2 + (cc + 1) * 128]

    def wk1w(cc):
        return pbuf[:, PB_WK1 + cc * 64: PB_WK1 + (cc + 1) * 64]

    def wk2w(cc):
        return pbuf[:, PB_WK2 + cc * 128: PB_WK2 + (cc + 1) * 128]

    def lntp(p):
        off = PB_LT0 if p == 0 else PB_LT1
        return pbuf[:, off: off + 1024].rearrange("p (cc t) -> p cc t", cc=8)

    def wdT(hc, f0, f1):
        return pbuf[:, hc * C + f0: hc * C + f1]

    Eb1 = pbuf[:, PB_WK2:PB_WK2 + 1024]

    def qqh(hh):
        return pbuf[:, PB_WV + hh * 512: PB_WV + (hh + 1) * 512]

    Eb0 = pbuf[:, PB_K1:PB_K1 + 1024]
    bo_b = pbuf[:, PB_K1 + 1024:PB_K1 + 2048]
    lnsq = [pbuf[:, PB_LT0:PB_LT0 + 512], pbuf[:, PB_LT0 + 512:PB_LT0 + 1024]]
    sqtb = pbuf[:, PB_LT1:PB_LT1 + 512]

    SA = es.enter_context(nc.psum_tensor("SA", [128, 1024], F32))
    SA2 = es.enter_context(nc.psum_tensor("SA2", [128, 1024], F32))
    SA3 = es.enter_context(nc.psum_tensor("SA3", [128, 1024], F32))
    accb = [es.enter_context(nc.psum_tensor("acc0", [128, 512], F32)),
            es.enter_context(nc.psum_tensor("acc1", [128, 512], F32))]

    G = Gen(nc, es)
    sGw = Sem(nc, es, "sGw")
    sSx = [Sem(nc, es, "sSx0"), Sem(nc, es, "sSx1")]
    sSy = Sem(nc, es, "sSy")
    sTp = [Sem(nc, es, "sTp0"), Sem(nc, es, "sTp1")]
    sT2 = Sem(nc, es, "sT2")
    sWg = Sem(nc, es, "sWg")
    sWu = Sem(nc, es, "sWu")
    sHw = [Sem(nc, es, "sHw%d" % i) for i in range(NW)]
    sHl = Sem(nc, es, "sHl")
    sTap = Sem(nc, es, "sTap")

    from concourse.hw_specs import get_activation_tables
    tabs = list(get_activation_tables(nc.m.arch).items())

    def set_with(*fs):
        for i, (nm, funcs) in enumerate(tabs):
            if all(f in funcs for f in fs):
                return i
        raise ValueError(f"no activation table set with {fs}")
    NLX_ID = set_with(AF.Exp, AF.Ln)
    SILU_ID = set_with(AF.Silu)

    def load_table(tid):
        def f(e):
            return e.add_instruction(mybir.InstLoadActFuncSet(
                name=nc.get_next_instruction_name(), act_func_set_id=tid,
                ins=[], outs=[]))
        return f

    op, group = G.op, G.group

    def tap(buf_names, out_ap, in_ap):
        if not dbg:
            return
        def f(e):
            with nc.allow_non_contiguous_dma(reason="debug tap"):
                return e.dma_start(out=out_ap, in_=in_ap)
        op("gpsimd", f, reads=list(buf_names), dma_sem=sTap)

    G.ops["scalar"].append((load_table(NLX_ID), [], None))
    op("gpsimd", lambda e: e.dma_start(out=xp[0][:], in_=x_d[0:128, :]),
       writes=["xp0"], dma_sem=sSx[0])
    op("gpsimd", lambda e: e.dma_start(out=xp[1][:], in_=x_d[128:256, :]),
       writes=["xp1"], dma_sem=sSx[1])
    op("gpsimd", lambda e: e.memset(v_sb[:, :, :, HD:HD + 1], 1.0), writes=["vones"])
    op("gpsimd", lambda e: e.memset(ones128[:], 1.0), writes=["ones128"])
    op("vector", lambda e: e.memset(epsb, EPS), writes=["statc"])
    op("vector", lambda e: e.memset(tinyb, 1e-30), writes=["statc"])
    op("vector", lambda e: e.memset(sh20, float(20.0 * np.log(2.0))), writes=["statc"])
    op("gpsimd", lambda e: e.memset(kk_sb[:], 0.0), writes=["kk"])
    op("gpsimd", lambda e: e.memset(scrbf[:], 0.0), writes=["scrbf"])
    op("gpsimd", lambda e: e.memset(pbuf[:, PB_LT1:PB_LT1 + 512], 0.0), writes=["sqtb"])
    op("gpsimd", lambda e: e.memset(q1x[64:128, :], 0.0), writes=["q1x"])
    op("gpsimd", lambda e: e.memset(Dfull[:], 0.0), writes=["Dfull"])
    op("gpsimd", lambda e: e.memset(rrpad[:], 0.0), writes=["scrbf_rr"])
    op("gpsimd", lambda e: e.memset(onesel[:], 0.0), writes=["onesel"])
    op("gpsimd", lambda e: e.memset(onesel[64:65, :], 1.0), writes=["onesel"])
    op("gpsimd", lambda e: e.memset(pbuf[64:128, PB_K1:PB_K1 + N], 0.0), writes=["k1Tb"])
    op("gpsimd", lambda e: e.memset(pbuf[64:80, PB_K1:PB_K1 + N], -1.0), writes=["k1Tb"])
    for nm, dst, srcd in [("ident", ident_sb[:], ident_d), ("identb", identb_sb[:], identb_d),
                          ("mask2", mask2_sb[:], mask2_d), ("maskqm", maskqm_sb[:], maskqm_d),
                          ("rep1a", rep1a_sb[:], rep1a_d), ("rep2", rep2_sb[:], rep2_d),
                          ("repadp", repadp_sb[:], repadp_d),
                          ("bd1", bd1_sb[:], bd1_d), ("bd2", bd2_sb[:], bd2_d),
                          ("bo_bf", bo_bf[:], bo_d)]:
        op("gpsimd", (lambda d, s: lambda e: e.dma_start(out=d, in_=s))(dst, srcd),
           writes=[nm], dma_sem=sGw)
    op("gpsimd", lambda e: e.dma_start(
        out=pbuf[:, PB_WV:PB_WV + 8 * C].rearrange("p (cc f) -> p cc f", cc=8),
        in_=wv_d.rearrange("(cc p) f -> p cc f", p=128)), writes=["wvT"], dma_sem=sGw)
    op("gpsimd", lambda e: e.dma_start(
        out=pbuf[:, PB_WQ1:PB_WQ1 + 512].rearrange("p (cc f) -> p cc f", cc=8),
        in_=wq1_d.rearrange("(cc p) f -> p cc f", p=128)), writes=["wq1w"], dma_sem=sGw)
    op("gpsimd", lambda e: e.dma_start(
        out=pbuf[:, PB_WQ2:PB_WQ2 + 1024].rearrange("p (cc f) -> p cc f", cc=8),
        in_=wq2_d.rearrange("(cc p) f -> p cc f", p=128)), writes=["wq2w"], dma_sem=sGw)
    op("gpsimd", lambda e: e.dma_start(
        out=pbuf[:, PB_WK1:PB_WK1 + 512].rearrange("p (cc f) -> p cc f", cc=8),
        in_=wk1_d.rearrange("(cc p) f -> p cc f", p=128)), writes=["wk1w"], dma_sem=sGw)
    op("gpsimd", lambda e: e.dma_start(
        out=pbuf[:, PB_WK2:PB_WK2 + 1024].rearrange("p (cc f) -> p cc f", cc=8),
        in_=wk2_d.rearrange("(cc p) f -> p cc f", p=128)), writes=["wk2w"], dma_sem=sGw)
    op("gpsimd", lambda e: e.dma_start(out=wot_sb[:], in_=wo_d.rearrange("(cc p) f -> p cc f", p=128)),
       writes=["wot"], dma_sem=sGw)
    for _nm in ("ident", "identb", "mask2", "maskqm", "rep1a", "rep2", "repadp",
                "bd1", "bd2", "bo_bf", "wvT", "wq1w", "wq2w", "wk1w", "wk2w", "wot"):
        G.B(_nm).did_write((sGw, sGw.v))

    def ln_full(p, xname, x_ap, outname, out_ap):
        st = statsv[p]
        sn = "stats%d" % p
        s1, s2, s2b = st[:, 0:1], st[:, 1:2], st[:, 2:3]
        mst, vst, rstd, nmr = st[:, 3:4], st[:, 4:5], st[:, 5:6], st[:, 6:7]
        sq = lnsq[p % 2]
        op("vector", (lambda a=s1: lambda e: e.reduce_sum(a, x_ap, axis=mybir.AxisListType.X))(),
           reads=[xname], writes=[sn])
        op("scalar", (lambda a=s2, q=sq: lambda e: e.activation(
            q, x_ap[:, 0:512], AF.Square, accum_out=a))(),
           reads=[xname], writes=[sn, "lnsq%d" % (p % 2)])
        op("scalar", (lambda a=s2b, q=sq: lambda e: e.activation(
            q, x_ap[:, 512:1024], AF.Square, accum_out=a))(),
           reads=[xname], writes=[sn, "lnsq%d" % (p % 2)])
        op("vector", (lambda a=s2, b=s2b: lambda e: e.tensor_tensor(a, a, b, AluOpType.add))(),
           reads=[sn], writes=[sn])
        op("vector", (lambda a=mst, b=s1: lambda e: e.tensor_scalar_mul(a, b, 1.0 / C))(),
           reads=[sn], writes=[sn])
        op("vector", (lambda a=nmr, b=mst: lambda e: e.tensor_tensor(a, b, b, AluOpType.mult))(),
           reads=[sn], writes=[sn])
        op("vector", (lambda a=vst, b=s2: lambda e: e.tensor_scalar_mul(a, b, 1.0 / C))(),
           reads=[sn], writes=[sn])
        op("vector", (lambda a=vst, b=nmr: lambda e: e.tensor_tensor(a, a, b, AluOpType.subtract))(),
           reads=[sn], writes=[sn])
        op("scalar", (lambda a=vst: lambda e: e.activation(a, a, AF.Ln, bias=epsb))(),
           reads=[sn, "statc"], writes=[sn])
        op("scalar", (lambda a=rstd, b=vst: lambda e: e.activation(a, b, AF.Exp, scale=-0.5))(),
           reads=[sn], writes=[sn])
        op("vector", (lambda a=nmr, b=mst, c=rstd: lambda e: e.tensor_scalar(
            a, b, c, -1.0, AluOpType.mult, AluOpType.mult))(),
           reads=[sn], writes=[sn])
        op("vector", (lambda c=rstd, d=nmr: lambda e: e.tensor_scalar(
            out_ap, x_ap, c, d, AluOpType.mult, AluOpType.add))(),
           reads=[xname, sn], writes=[outname])

    for vol in range(4):
        vs, ve = vol * 512, (vol + 1) * 512
        for t4 in range(4):
            tt = 4 * vol + t4
            pi = tt % 2
            xn, ln_ = "xp%d" % pi, "lnp%d" % pi
            ln_full(tt % 4, xn, xp[pi][:], ln_, lnp[pi][:])
            if tt + 2 < NTT:
                op("sync", (lambda t, p: lambda e: e.dma_start(
                    out=xp[p][:], in_=x_d[t * 128:(t + 1) * 128, :]))(tt + 2, pi),
                   writes=["xp%d" % pi], dma_sem=sSx[pi])
            op("sync", (lambda p, t4_: lambda e: e.dma_start_transpose(
                lnt4[:, :, t4_ * 128:(t4_ + 1) * 128], lnp[p][:]))(pi, t4),
               reads=[ln_], writes=["lnt4"], dma_sem=sTp[pi])
        group("tensor", [
            (lambda c: lambda e: e.matmul(SA[0:64, 0:512], wq1w(c), lnt4[:, c, :],
                                          start=(c == 0), stop=(c == 7), skip_group_check=True))(cc)
            for cc in range(8)], reads=["wq1w", "lnt4"], writes=["SA"])
        group("tensor", [
            (lambda c: lambda e: e.matmul(SA[:, 512:1024], wq2w(c), lnt4[:, c, :],
                                          start=(c == 0), stop=(c == 7), skip_group_check=True))(cc)
            for cc in range(8)], reads=["wq2w", "lnt4"], writes=["SA"])
        group("tensor", [
            (lambda c: lambda e: e.matmul(SA2[0:64, 0:512], wk1w(c), lnt4[:, c, :],
                                          start=(c == 0), stop=(c == 7), skip_group_check=True))(cc)
            for cc in range(8)], reads=["wk1w", "lnt4"], writes=["SA2"])
        group("tensor", [
            (lambda c: lambda e: e.matmul(SA2[:, 512:1024], wk2w(c), lnt4[:, c, :],
                                          start=(c == 0), stop=(c == 7), skip_group_check=True))(cc)
            for cc in range(8)], reads=["wk2w", "lnt4"], writes=["SA2"])
        op("vector", (lambda a, b: lambda e: e.tensor_copy(q1x[0:64, a:b], SA[0:64, 0:512]))(vs, ve),
           reads=["SA"], writes=["q1x"])
        op("vector", (lambda a, b: lambda e: e.tensor_copy(q2T[:, a:b], SA[:, 512:1024]))(vs, ve),
           reads=["SA"], writes=["q2T"])
        op("vector", (lambda a, b: lambda e: e.tensor_copy(k1T(a, b), SA2[0:64, 0:512]))(vs, ve),
           reads=["SA2"], writes=["k1Tb"])
        op("vector", (lambda a, b: lambda e: e.tensor_copy(k2T(a, b), SA2[:, 512:1024]))(vs, ve),
           reads=["SA2"], writes=["k2Tb"])
        for t4 in range(4):
            tt = 4 * vol + t4
            vq, vqn = (SA3, "SA3") if t4 % 2 == 0 else (SA, "SA")
            group("tensor", [
                (lambda c, hf, t_, p=vq: lambda e: e.matmul(
                    p[:, hf * 512:(hf + 1) * 512], lnt4[:, c, t_ * 128:(t_ + 1) * 128],
                    wvT(c, hf * 512, (hf + 1) * 512),
                    start=(c == 0), stop=(c == 7), skip_group_check=True))(cc, half, t4)
                for half in range(2) for cc in range(8)],
                reads=["lnt4", "wvT"], writes=[vqn])
            op("vector", (lambda t, p=vq: lambda e: e.tensor_copy(
                v_sb[:, t, :, 0:HD], p[:, 0:1024].rearrange("p (h d) -> p h d", h=H)))(tt),
               reads=[vqn], writes=["vbuf"])
        sq_ap, sq_nm = (scrbf[:, 0:512], "scrbf") if vol % 2 == 0 else (sqtb[:], "sqtb")
        for (srcfn, srcnm, sp, bdnm, bdap, kind) in (
                (lambda a=vs, b=ve: q1x[0:64, a:b], "q1x", 64, "bd1", bd1_sb[:], "q1"),
                (lambda a=vs, b=ve: q2T[:, a:b], "q2T", 128, "bd2", bd2_sb[:], "q2"),
                (lambda a=vs, b=ve: k1T(a, b), "k1Tb", 64, "bd1", bd1_sb[:], "k1"),
                (lambda a=vs, b=ve: k2T(a, b), "k2Tb", 128, "bd2", bd2_sb[:], "k2")):
            op("scalar", (lambda s_, p_, q=sq_ap: lambda e: e.activation(q[0:p_, :], s_(), AF.Square))(srcfn, sp),
               reads=[srcnm], writes=[sq_nm])
            op("tensor", (lambda b_, q=sq_ap: lambda e: e.matmul(
                accb[0][0:16, 0:512], b_, q[:, :], start=True, stop=True,
                skip_group_check=True))(bdap),
               reads=[sq_nm, bdnm], writes=["acc0"])
            if kind in ("q1", "q2"):
                i = 0 if kind == "q1" else 1
                op("vector", (lambda ii, a, b: lambda e: e.tensor_copy(
                    nq(ii, a, b), accb[0][0:16, 0:512]))(i, vs, ve),
                   reads=["acc0"], writes=["nq%d" % i])
            elif kind == "k1":
                op("vector", (lambda a, b: lambda e: e.tensor_copy(
                    nkp(a, b), accb[0][0:16, 0:512]))(vs, ve),
                   reads=["acc0"], writes=["nkpb"])
            else:
                op("vector", (lambda a, b: lambda e: e.tensor_tensor(
                    nkp(a, b), nkp(a, b), accb[0][0:16, 0:512], AluOpType.mult))(vs, ve),
                   reads=["acc0", "nkpb"], writes=["nkpb"])
        op("vector", (lambda a, b, q=sq_ap: lambda e: e.tensor_tensor(
            q[0:64, :], q1x[0:64, a:b], k1T(a, b), AluOpType.mult))(vs, ve),
           reads=["q1x", "k1Tb"], writes=[sq_nm])
        op("tensor", (lambda q=sq_ap: lambda e: e.matmul(
            accb[0][0:16, 0:512], bd1_sb[:], q[:, :], start=True, stop=True,
            skip_group_check=True))(),
           reads=[sq_nm, "bd1"], writes=["acc0"])
        op("vector", (lambda a, b: lambda e: e.tensor_copy(dg(a, b), accb[0][0:16, 0:512]))(vs, ve),
           reads=["acc0"], writes=["dgb"])
        op("vector", (lambda a, b, q=sq_ap: lambda e: e.tensor_tensor(
            q[:, :], q2T[:, a:b], k2T(a, b), AluOpType.mult))(vs, ve),
           reads=["q2T", "k2Tb"], writes=[sq_nm])
        op("tensor", (lambda q=sq_ap: lambda e: e.matmul(
            accb[0][0:16, 0:512], bd2_sb[:], q[:, :], start=True, stop=True,
            skip_group_check=True))(),
           reads=[sq_nm, "bd2"], writes=["acc0"])
        op("vector", (lambda a, b: lambda e: e.tensor_tensor(
            dg(a, b), dg(a, b), accb[0][0:16, 0:512], AluOpType.mult))(vs, ve),
           reads=["acc0", "dgb"], writes=["dgb"])

    Dw0 = Dw[0:16, 0:512]
    Dw1 = Dw[0:16, 512:1024]
    SAb = [(SA, "SA"), (SA2, "SA2"), (SA3, "SA3")]
    Ebb, Ebn = [Eb0, Eb1], ["E0", "E1"]
    ktpar = [0]
    hw_marks = {}
    def dpre_units(qb, bt=None, bn=None):
        q0 = qb * WQ
        if bt is None:
            bt, bn = SA3, "SA3"

        def emit_build(g8):
            op("tensor", (lambda gg, q: lambda e: e.matmul(
                bt[:, 0:512], rep1a_sb[:, gg, :], q1x[:, q:q + WQ],
                start=True, stop=True, skip_group_check=True))(g8, q0),
               reads=["rep1a", "q1x"], writes=[bn])
            op("tensor", (lambda gg, q: lambda e: e.matmul(
                bt[:, 512:1024], rep2_sb[:, gg, :], q2T[:, q:q + WQ],
                start=True, stop=True, skip_group_check=True))(g8, q0),
               reads=["rep2", "q2T"], writes=[bn])
            op("vector", lambda e: e.tensor_copy(t1[:], bt[:, 0:512]),
               reads=[bn], writes=["t1"])
            op("vector", (lambda gg: lambda e: e.tensor_tensor(
                qqh(2 * gg)[0:32, :], t1[0:32, :], bt[0:32, 512:1024], AluOpType.mult))(g8),
               reads=["t1", bn], writes=["qqh"])
            op("vector", (lambda gg: lambda e: e.tensor_copy(
                qqh(2 * gg)[32:33, :], t1[32:33, :]))(g8),
               reads=["t1"], writes=["qqh"])
            op("vector", (lambda gg: lambda e: e.tensor_tensor(
                qqh(2 * gg + 1)[64:96, :], t1[64:96, :], bt[64:96, 512:1024], AluOpType.mult))(g8),
               reads=["t1", bn], writes=["qqh"])
            op("vector", (lambda gg: lambda e: e.tensor_copy(
                qqh(2 * gg + 1)[96:97, :], t1[96:97, :]))(g8),
               reads=["t1"], writes=["qqh"])

        def emit_minis(g8):
            for j2 in range(2):
                hh = 2 * g8 + j2
                mq, mn = accb[hh % 2], "acc%d" % (hh % 2)
                fns = []
                for qt in range(4):
                    masked = (qb == 0 and qt == 0)
                    fns.append((lambda hx=hh, t=qt, m=masked, p=mq: lambda e: e.matmul(
                        p[:, t * 128:(t + 1) * 128],
                        qqh(hx)[:, t * 128:(t + 1) * 128], kk_sb[:, hx // 2, 0:128],
                        start=True, stop=not m, skip_group_check=True))())
                    if masked:
                        fns.append((lambda p=mq: lambda e: e.matmul(
                            p[:, 0:128], identb_sb[:], maskqm_sb[:],
                            start=False, stop=True, skip_group_check=True))())
                group("tensor", fns, reads=["qqh", "kk", "identb", "maskqm"], writes=[mn])
                for qt in range(4):
                    op("vector", (lambda hx=hh, t=qt, p=mq: lambda e: e.reduce_max(
                        Lmq[:, t, hx:hx + 1], p[:, t * 128:(t + 1) * 128],
                        axis=mybir.AxisListType.X))(),
                       reads=[mn], writes=["Lmq"])

        units = []
        for g8 in range(8):
            units.append((lambda gg=g8: emit_build(gg)))
            if g8 > 0:
                units.append((lambda gg=g8 - 1: emit_minis(gg)))
        units.append(lambda: emit_minis(7))

        def emit_dchain():
            for qt in range(4):
                op("tensor", (lambda t: lambda e: e.transpose(
                    bt[0:16, t * 128:(t + 1) * 128], Lmq[:, t, :], ident_sb[:]))(qt),
                   reads=["Lmq", "ident"], writes=[bn])
            op("vector", lambda e: e.tensor_copy(Dw0, bt[0:16, 0:512]),
               reads=[bn], writes=["Dwb"])
            op("vector", (lambda q: lambda e: e.tensor_tensor(
                ntmp, nq(0, q, q + WQ), nkp_fin(q, q + WQ), AluOpType.mult))(q0),
               reads=["nq0", NKP_FINAL], writes=["scr32"])
            op("scalar", lambda e: e.activation(ntmp, ntmp, AF.Ln, bias=tinyb[0:16]),
               reads=["scr32", "statc"], writes=["scr32"])
            op("scalar", lambda e: e.activation(ntmp, ntmp, AF.Exp, scale=0.5),
               reads=["scr32"], writes=["scr32"])
            op("vector", lambda e: e.tensor_tensor(Dw1, ntmp, Dw0, AluOpType.max),
               reads=["scr32", "Dwb"], writes=["Dwb"])
            op("vector", (lambda q: lambda e: e.tensor_tensor(
                Dw0, Dw0, dg(q, q + WQ), AluOpType.max))(q0),
               reads=["Dwb", "dgb"], writes=["Dwb"])
            op("vector", lambda e: e.tensor_scalar(Dw0, Dw0, float(C2), None, AluOpType.add),
               reads=["Dwb"], writes=["Dwb"])
            op("vector", lambda e: e.tensor_tensor(Dw0, Dw0, Dw1, AluOpType.min),
               reads=["Dwb"], writes=["Dwb"])
            op("vector", lambda e: e.tensor_scalar(Dw1, Dw1, float(C1), None, AluOpType.subtract),
               reads=["Dwb"], writes=["Dwb"])
            op("vector", lambda e: e.tensor_tensor(Dw0, Dw0, Dw1, AluOpType.max),
               reads=["Dwb"], writes=["Dwb"])
            op("vector", lambda e: e.tensor_scalar(Dw0, Dw0, float(DFLOOR), None, AluOpType.max),
               reads=["Dwb"], writes=["Dwb"])
            op("vector", lambda e: e.tensor_copy(Dfull[0:16, :], Dw0),
               reads=["Dwb"], writes=["Dfull"])
        units.append(emit_dchain)

        def emit_augfix(g8):
            op("tensor", (lambda gg: lambda e: e.matmul(
                bt[:, 0:512], repadp_sb[:, gg, :], Dfull[:, :],
                start=True, stop=True, skip_group_check=True))(g8),
               reads=["repadp", "Dfull"], writes=[bn])
            op("vector", (lambda gg: lambda e: e.tensor_copy(
                qqh(2 * gg)[32:33, :], bt[32:33, 0:512]))(g8),
               reads=[bn], writes=["qqh"])
            op("vector", (lambda gg: lambda e: e.tensor_copy(
                qqh(2 * gg + 1)[96:97, :], bt[96:97, 0:512]))(g8),
               reads=[bn], writes=["qqh"])
        for g8 in range(8):
            units.append((lambda gg=g8: emit_augfix(gg)))
        return units


    op("gpsimd", lambda e: e.memset(pbuf[:, PB_WV:PB_WV + 16 * 512], 0.0),
       writes=["qqh"], extra_waits=list(G.B("wvT").write_waits().items()))
    scan_ops = []
    scan_ops.append(lambda: op("vector", lambda e: e.tensor_tensor(
        nq(0, 0, N), nq(0, 0, N), nq(1, 0, N), AluOpType.mult),
        reads=["nq0", "nq1"], writes=["nq0"]))
    scan_ops.append(lambda: op("vector", lambda e: e.memset(nkp(0, 128), 0.0),
                               reads=["nkpb"], writes=["nkpb"]))
    sbufs = ["nkpb", "nq1"]
    step = 1
    while step < N:
        sfn, dfn = (nkp, ping2) if sbufs[0] == "nkpb" else (ping2, nkp)
        scan_ops.append((lambda s_, d_, st, rd, wr: lambda: op(
            "vector", lambda e: e.tensor_tensor(
                d_(st, N), s_(st, N), s_(0, N - st), AluOpType.max),
            reads=[rd], writes=[wr]))(sfn, dfn, step, sbufs[0], sbufs[1]))
        scan_ops.append((lambda s_, d_, st, rd, wr: lambda: op(
            "vector", lambda e: e.tensor_copy(d_(0, st), s_(0, st)),
            reads=[rd], writes=[wr]))(sfn, dfn, step, sbufs[0], sbufs[1]))
        sbufs = [sbufs[1], sbufs[0]]
        step *= 2
    NKP_FINAL = sbufs[0]
    nkp_fin = nkp if NKP_FINAL == "nkpb" else ping2
    u0 = dpre_units(0, SA, "SA")

    for g8 in range(8):
        for qc in range(4):
            a, b = qc * 512, (qc + 1) * 512
            op("tensor", (lambda gg, aa, bb: lambda e: e.matmul(
                SA3[:, 0:512], rep1a_sb[:, gg, :], pbuf[0:128, PB_K1 + aa:PB_K1 + bb],
                start=True, stop=True, skip_group_check=True))(g8, a, b),
               reads=["rep1a", "k1Tb"], writes=["SA3"])
            op("tensor", (lambda gg, aa, bb: lambda e: e.matmul(
                SA3[:, 512:1024], rep2_sb[:, gg, :], k2T(aa, bb),
                start=True, stop=True, skip_group_check=True))(g8, a, b),
               reads=["rep2", "k2Tb"], writes=["SA3"])
            op("scalar", lambda e: e.activation(t1[:], SA3[:, 0:512], AF.Copy),
               reads=["SA3"], writes=["t1"])
            op("vector", (lambda gg, aa, bb: lambda e: e.tensor_tensor(
                kk_sb[0:32, gg, aa:bb], t1[0:32, :], SA3[0:32, 512:1024], AluOpType.mult))(g8, a, b),
               reads=["t1", "SA3"], writes=["kk"])
            op("scalar", (lambda gg, aa, bb: lambda e: e.activation(
                kk_sb[32:33, gg, aa:bb], t1[32:33, :], AF.Copy))(g8, a, b),
               reads=["t1"], writes=["kk"])
            op("vector", (lambda gg, aa, bb: lambda e: e.tensor_tensor(
                kk_sb[64:96, gg, aa:bb], t1[64:96, :], SA3[64:96, 512:1024], AluOpType.mult))(g8, a, b),
               reads=["t1", "SA3"], writes=["kk"])
            op("scalar", (lambda gg, aa, bb: lambda e: e.activation(
                kk_sb[96:97, gg, aa:bb], t1[96:97, :], AF.Copy))(g8, a, b),
               reads=["t1"], writes=["kk"])
        u0.pop(0)()
        if g8 > 0:
            u0.pop(0)()
        while scan_ops and len(scan_ops) > 24 * (7 - g8) // 8:
            scan_ops.pop(0)()
    u0.pop(0)()
    assert len(u0) == 9, len(u0)
    assert not scan_ops

    group("tensor", [
        (lambda hf: lambda e: e.matmul(SA[:, hf * 512:(hf + 1) * 512], ones128[:],
                                       bo_bf[:, hf * 512:(hf + 1) * 512],
                                       start=True, stop=True, skip_group_check=True))(hf)
        for hf in range(2)],
        reads=["ones128", "bo_bf"], writes=["SA"])
    op("vector", lambda e: e.tensor_copy(bo_b[:], SA[:, 0:1024]), reads=["SA"], writes=["bo_b"])

    if dbg:
        tap(["q1x"], t_q1x, q1x[0:80, :])
        tap(["q2T"], t_q2T, q2T[:])
        tap(["dgb"], t_dg, dg(0, N))
        tap([NKP_FINAL], t_nkp, nkp_fin(0, N))
        tap(["kk"], t_kk, kk_sb[:].rearrange("p g n -> p (g n)"))
        tap(["nq0"], t_nq[0:16, 0:N], nq(0, 0, N))
        tap(["nq1"], t_nq[0:16, N:2 * N], nq(1, 0, N))
        tap(["vbuf"], t_v, v_sb[:].rearrange("p a b c -> p (a b c)"))

    for _s, _v in G.B("wk2w").write_waits().items():
        G.B("E1").r[_s] = max(G.B("E1").r.get(_s, 0), _v)
    for _nm in ("E0", "bo_b"):
        for _s, _v in G.B("k1Tb").write_waits().items():
            G.B(_nm).r[_s] = max(G.B(_nm).r.get(_s, 0), _v)
    for _nm in ("wgi0", "wgi1"):
        for _s, _v in G.B("lnt4").write_waits().items():
            G.B(_nm).r[_s] = max(G.B(_nm).r.get(_s, 0), _v)
    for u in u0:
        u()

    for qb in range(NW):
        q0 = qb * WQ
        nkt = 4 * qb + 4
        if dbg and qb == 0:
            tap(["qqh"], t_qq, pbuf[:, PB_WV:PB_WV + 8 * WQ])
        def ld_wgu(c):
            b = c % 3
            b2 = c % 2
            op("sync", (lambda cx, bx: lambda e: e.dma_start(
                out=wgi[bx][:], in_=wg_d[:, cx * 128:(cx + 1) * 128].rearrange("(cc p) f -> p cc f", p=128)))(c, b2),
               writes=["wgi%d" % b2], dma_sem=sWg)
            op("vector", (lambda bx, bi: lambda e: e.tensor_copy(
                wgb[bx][:], wgi[bi][:]))(b, b2),
               reads=["wgi%d" % b2], writes=["wgb%d" % b])
            op("gpsimd", (lambda cx, bx: lambda e: e.dma_start(
                out=wub[bx][:], in_=wu_d[:, cx * 128:(cx + 1) * 128].rearrange("(cc p) f -> p cc f", p=128)))(c, b),
               writes=["wub%d" % b], dma_sem=sWu)

        deferred = [None]

        def flush_deferred():
            if deferred[0] is not None:
                for f in deferred[0]:
                    f()
                deferred[0] = None

        for h in range(H):
            a = accb[h % 2]
            an = "acc%d" % (h % 2)
            npairs = (nkt + 1) // 2

            def emit_S(ktp, h=h, a=a, an=an):
                kts = [kt for kt in (2 * ktp, 2 * ktp + 1) if kt < nkt]
                SAc, SAnm = SAb[(ktpar[0] + ktp) % 3]
                fns = []
                rd = ["kk", "qqh"]
                for i, kt in enumerate(kts):
                    d = kt - 4 * qb
                    qlo = 128 * max(0, d)
                    diag = d >= 0
                    fns.append((lambda hx=h, k=kt, ql=qlo, ii=i, p=SAc, dgm=diag:
                                lambda e: e.matmul(
                                    p[:, ii * 512 + ql: ii * 512 + 512],
                                    kk_sb[:, hx // 2, k * 128:(k + 1) * 128],
                                    qqh(hx)[:, ql:WQ],
                                    start=True, stop=not dgm, skip_group_check=True))())
                    if diag:
                        fns.append((lambda ql=qlo, ii=i, p=SAc: lambda e: e.matmul(
                            p[:, ii * 512 + ql: ii * 512 + ql + 128],
                            identb_sb[:], mask2_sb[:],
                            start=False, stop=True, skip_group_check=True))())
                        rd = ["kk", "qqh", "identb", "mask2"]
                group("tensor", fns, reads=rd, writes=[SAnm])

            def emit_exp(ktp, h=h):
                kts = [kt for kt in (2 * ktp, 2 * ktp + 1) if kt < nkt]
                SAc, SAnm = SAb[(ktpar[0] + ktp) % 3]
                pp = (ktpar[0] + ktp) % 2
                Ebc, Ebnm = Ebb[pp], Ebn[pp]
                wid = 512 * len(kts)
                op("scalar", (lambda w, p, eb: lambda e: e.activation(
                    eb[:, 0:w], p[:, 0:w], AF.Exp, scale=SCALE))(wid, SAc, Ebc),
                   reads=[SAnm], writes=[Ebnm])
                if dbg and qb == 0 and h == 0 and ktp == 0:
                    tap([Ebnm], t_E, Ebc[:])

            def emit_V(ktp, h=h, a=a, an=an):
                kts = [kt for kt in (2 * ktp, 2 * ktp + 1) if kt < nkt]
                pp = (ktpar[0] + ktp) % 2
                Ebc, Ebnm = Ebb[pp], Ebn[pp]
                for i, kt in enumerate(kts):
                    qlo = 128 * max(0, kt - 4 * qb)
                    mk = op("tensor", (lambda k, ii, ql, hx, aa, nk, eb: lambda e: e.matmul(
                        aa[0:HD + 1, ql:WQ], v_sb[:, k, hx, :],
                        eb[:, ii * 512 + ql: ii * 512 + 512],
                        start=(k == 0), stop=(k == nk - 1),
                        skip_group_check=True))(kt, i, qlo, h, a, nkt, Ebc),
                        reads=[Ebnm, "vbuf", "vones"],
                        writes=([an] if kt == 0 else []))
                    if kt == nkt - 1:
                        G.B(an).did_write(mk)

            emit_S(0)
            if npairs > 1:
                emit_S(1)
            flush_deferred()
            for t in range(npairs):
                if t + 2 < npairs:
                    emit_S(t + 2)
                emit_exp(t)
                emit_V(t)
            ktpar[0] += npairs
            if h % 2 == 0:
                op("scalar", lambda e: e.activation(scr32[64:65, 0:512], accb[0][HD:HD + 1, :],
                                                    AF.Ln, bias=statc[64:65, 1:2]),
                   reads=["acc0", "statc"], writes=["scr32"])
                op("scalar", lambda e: e.activation(rrpad[64:65, 0:512], scr32[64:65, 0:512],
                                                    AF.Exp, scale=-1.0),
                   reads=["scr32"], writes=["scrbf_rr"])
            else:
                bq, bqn = SAb[(ktpar[0] + 2) % 3]
                op("scalar", (lambda p=bq, n=bqn: lambda e: e.activation(
                    p[64:65, 0:512], accb[1][HD:HD + 1, :],
                    AF.Ln, bias=statc[64:65, 1:2]))(),
                   reads=["acc1", "statc"], writes=[bqn])
                op("scalar", (lambda p=bq: lambda e: e.activation(
                    rrpad[64:65, 512:1024], p[64:65, 0:512],
                    AF.Exp, scale=-1.0))(),
                   reads=[bqn], writes=["scrbf_rr"])

                def fin_pair(h=h, bq=bq, bqn=bqn):
                    for hx in (h - 1, h):
                        co = 512 * (hx % 2)
                        aa, an2 = accb[hx % 2], "acc%d" % (hx % 2)
                        op("tensor", (lambda c=co, p=bq: lambda e: e.matmul(
                            p[0:64, 0:512], onesel[:], rrpad[:, c:c + 512],
                            start=True, stop=True, skip_group_check=True))(),
                           reads=["onesel", "scrbf_rr"], writes=[bqn])
                        op("vector", (lambda p=bq: lambda e: e.tensor_copy(
                            rrs, p[0:64, 0:512]))(),
                           reads=[bqn], writes=["scr32"])
                        if dbg and qb == 0 and hx == 0:
                            tap(["scr32"], t_rr, rrs)
                        op("vector", (lambda hy=hx, ab=aa: lambda e: e.tensor_tensor(
                            aoT[64 * (hy % 2):64 * (hy % 2) + 64, hy // 2, :],
                            ab[0:HD, :], rrs, AluOpType.mult))(),
                           reads=[an2, "scr32"], writes=["aoT"])
                deferred[0] = [fin_pair]
        flush_deferred()

        ld_wgu(0)
        ld_wgu(1)

        for tt in range(4):
            trow = q0 + tt * 128
            wps, wpn = (SA, "SA") if tt % 2 == 0 else (SA2, "SA2")
            fns = []
            for cc in range(8):
                for half in range(2):
                    def fo(c=cc, hf=half, t=tt, p=wps):
                        def f(e):
                            return e.matmul(p[:, hf * 512:(hf + 1) * 512],
                                            aoT[:, c, t * 128:(t + 1) * 128],
                                            wot_sb[:, c, hf * 512:(hf + 1) * 512],
                                            start=(c == 0), stop=(c == 7), skip_group_check=True)
                        return f
                    fns.append(fo())
            group("tensor", fns, reads=["wot", "aoT"], writes=[wpn])
            pi = tt % 2
            xn, ln_ = "xp%d" % pi, "lnp%d" % pi
            op("gpsimd", (lambda r, p: lambda e: e.dma_start(
                out=xp[p][:], in_=x_d[r:r + 128, :]))(trow, pi),
               writes=[xn], dma_sem=sSx[pi])
            for ch in range(2):
                op("vector", (lambda p, cch, wq=wps: lambda e: e.scalar_tensor_tensor(
                    scr32[:], wq[:, cch * 512:(cch + 1) * 512], 1.0,
                    xp[p][:, cch * 512:(cch + 1) * 512],
                    AluOpType.mult, AluOpType.add))(pi, ch),
                   reads=[wpn, xn], writes=["scr32"])
                op("vector", (lambda p, cch: lambda e: e.tensor_tensor(
                    xp[p][:, cch * 512:(cch + 1) * 512], scr32[:],
                    bo_b[:, cch * 512:(cch + 1) * 512], AluOpType.add))(pi, ch),
                   reads=["scr32", "bo_b"], writes=[xn])
            op("sync", (lambda r, p: lambda e: e.dma_start(out=y_d[r:r + 128, :], in_=xp[p][:]))(trow, pi),
               reads=[xn], writes=["yb_%d_%d" % (qb, tt)], dma_sem=sSy)
            ln_full(tt % 4, xn, xp[pi][:], ln_, lnp[pi][:])
            op("sync", (lambda t, p: lambda e: e.dma_start_transpose(
                ln2T[:, :, t * 128:(t + 1) * 128], lnp[p][:]))(tt, pi),
               reads=[ln_], writes=["ln2Tbuf"] if tt == 3 else [], dma_sem=sT2)
            if tt < 3:
                G.B("ln2Tbuf").did_write((sT2, sT2.v))
        if dbg and qb == 0:
            tap(["aoT"], t_ao, aoT[:].rearrange("p a b -> p (a b)"))
            tap(["ln2Tbuf"], t_ln2, ln2T[:].rearrange("p a b -> p (a b)"))

        G.ops["scalar"].append((load_table(SILU_ID), [], None))
        if qb + 1 < NW:
            _units = dpre_units(qb + 1)
            nbm = 2 * 8
            u_bm, u_dchain, u_aug = _units[0:nbm], _units[nbm], _units[nbm + 1:]
            assert len(u_aug) == 8
        else:
            u_bm, u_dchain, u_aug = [], None, []

        for hc in range(NHC):
            b = hc % 3
            if hc + 2 < NHC:
                ld_wgu(hc + 2)
            if 1 <= hc < 10:
                while u_bm and len(u_bm) > 16 * (9 - hc) // 9:
                    u_bm.pop(0)()
            if hc == 10 and u_dchain is not None:
                G.ops["scalar"].append((load_table(NLX_ID), [], None))
                u_dchain()
                G.ops["scalar"].append((load_table(SILU_ID), [], None))
            if hc >= 11:
                while u_aug and len(u_aug) > 8 * (NHC - 1 - hc) // 13:
                    u_aug.pop(0)()
            fq, fqn = (SA, "SA") if hc % 2 == 0 else (SA2, "SA2")
            group("tensor", [
                (lambda c, bx, p=fq: lambda e: e.matmul(p[:, 0:512], wgb[bx][:, c, :], ln2T[:, c, :],
                                                        start=(c == 0), stop=(c == 7), skip_group_check=True))(cc, b)
                for cc in range(8)], reads=["wgb%d" % b, "ln2Tbuf"], writes=[fqn])
            group("tensor", [
                (lambda c, bx, p=fq: lambda e: e.matmul(p[:, 512:1024], wub[bx][:, c, :], ln2T[:, c, :],
                                                        start=(c == 0), stop=(c == 7), skip_group_check=True))(cc, b)
                for cc in range(8)], reads=["wub%d" % b, "ln2Tbuf"], writes=[fqn])
            op("scalar", (lambda p=fq: lambda e: e.activation(
                t1[:], p[:, 0:512], AF.Silu, scale=QS))(),
               reads=[fqn], writes=["t1"])
            op("vector", (lambda p=fq: lambda e: e.tensor_tensor(
                hch, t1[:], p[:, 512:1024], AluOpType.mult))(),
               reads=["t1", fqn], writes=["scrbf"])
            if dbg and qb == 0 and hc == 0:
                tap(["scrbf"], t_h, hch)
            op("gpsimd", (lambda c, q: lambda e: e.dma_start(
                out=hts_d[c * 128:(c + 1) * 128, q:q + WQ], in_=hch))(hc, q0),
               reads=["scrbf"], writes=["htsbuf"] if hc == NHC - 1 else [], dma_sem=sHw[qb])
            if hc < NHC - 1:
                G.B("htsbuf").did_write((sHw[qb], sHw[qb].v))
        G.ops["scalar"].append((load_table(NLX_ID), [], None))
        hw_marks[qb] = (sHw[qb], sHw[qb].v)

    wdw = {}
    for nm in ("qqh", "k1Tb", "k2Tb", "nq0", "nq1", "wq1w", "wq2w", "wk1w",
               "E0", "bo_b", "lnsq0", "lnsq1", "sqtb", "dgb", "nkpb"):
        for s, v in G.B(nm).write_waits().items():
            wdw[s] = max(wdw.get(s, 0), v)
    op("gpsimd", lambda e: e.dma_start(
        out=pbuf[:, 0:NHC * C].rearrange("p (hc f) -> p hc f", hc=NHC),
        in_=wd_d.rearrange("(hc p) f -> p hc f", p=128)),
       writes=["wdT"], dma_sem=sGw, extra_waits=list(wdw.items()))

    hTt2 = ln2T[:].rearrange("p a b -> p (a b)")[:, 0:NHC * 128].rearrange(
        "p (hc t) -> p hc t", hc=NHC)
    for tq in range(NTT):
        trow = tq * 128
        htv = [hTt, hTt2][tq % 2]
        htn = ["aoT", "ln2Tbuf"][tq % 2]
        op("gpsimd", (lambda r, hv: lambda e: e.dma_start(
            out=hv, in_=hts_d[:, r:r + 128].rearrange("(hc p) t -> p hc t", p=128)))(trow, htv),
           reads=["htsbuf"], writes=[htn], dma_sem=sHl,
           extra_waits=[hw_marks[tq // 4]])
        pi = tq % 2
        xn = "xp%d" % pi
        op("sync", (lambda r, p: lambda e: e.dma_start(out=xp[p][:], in_=y_d[r:r + 128, :]))(trow, pi),
           writes=[xn], dma_sem=sSx[pi],
           extra_waits=[(sSy, sSy.v)])
        if dbg and tq == 0:
            tap([xn], t_x2r, xp[pi][:])
            tap(["aoT"], t_hT, hTt.rearrange("p a b -> p (a b)"))
        for ch in range(2):
            group("tensor", [
                (lambda c, cch, hv: lambda e: e.matmul(
                    SA2[:, 0:512], hv[:, c, :], wdT(c, cch * 512, (cch + 1) * 512),
                    start=(c == 0), stop=(c == NHC - 1), skip_group_check=True))(hc, ch, htv)
                for hc in range(NHC)], reads=[htn, "wdT"], writes=["SA2"])
            op("vector", (lambda cch, p: lambda e: e.scalar_tensor_tensor(
                scr32[:], SA2[:, 0:512], QS2,
                xp[p][:, cch * 512:(cch + 1) * 512],
                AluOpType.mult, AluOpType.add))(ch, pi),
               reads=["SA2", xn], writes=["scr32"])
            if dbg and tq == 0:
                tap(["scr32"], t_f0 if ch == 0 else t_f1, scr32[:])
            op("sync", (lambda r, cch: lambda e: e.dma_start(
                out=y_d[r:r + 128, cch * 512:(cch + 1) * 512], in_=scr32[:]))(trow, ch),
               reads=["scr32"], writes=["yf_%d_%d" % (tq, ch)], dma_sem=sSy)

    G.wait_only("sync", {sSy: sSy.v})
    G.wait_only("gpsimd", {sHl: sHl.v, sGw: sGw.v, sTap: sTap.v})

    with nc.Block() as block:
        def emitter(engname):
            def emit(eng):
                last = {}
                for fn, waits, inc in G.ops[engname]:
                    for s, v in waits:
                        if last.get(s.name, 0) < v:
                            eng.wait_ge(s.h, v)
                            last[s.name] = v
                    ins = fn(eng) if fn is not None else None
                    if inc is not None and ins is not None:
                        s, amt = inc
                        ins.then_inc(s.h, amt)
            return emit
        block.tensor(emitter("tensor"))
        block.scalar(emitter("scalar"))
        block.vector(emitter("vector"))
        block.gpsimd(emitter("gpsimd"))
        block.sync(emitter("sync"))

    es.close()
    return nc


_NC_CACHE = {}


def _get_nc():
    if "nc" not in _NC_CACHE:
        _NC_CACHE["nc"] = build_nc()
    return _NC_CACHE["nc"]


def make_consts():
    ident = np.eye(128, dtype=np.float32)
    mask2 = (-1e9 * np.tril(np.ones((128, 128), np.float32), -1)).astype(np.float32)
    maskqm = (-1e9 * np.triu(np.ones((128, 128), np.float32), 1)).astype(np.float32)
    rep1a = np.zeros((128, 8, 128), np.float32)
    rep2 = np.zeros((128, 8, 128), np.float32)
    repadp = np.zeros((128, 8, 128), np.float32)
    for g8 in range(8):
        for j2 in range(2):
            hh = 2 * g8 + j2
            for r in range(R1):
                for s in range(R2):
                    rep1a[R1 * hh + r, g8, 64 * j2 + R2 * r + s] = 1.0
                    rep2[R2 * hh + s, g8, 64 * j2 + R2 * r + s] = 1.0
            rep1a[64 + hh, g8, 64 * j2 + 32] = 1.0
        repadp[2 * g8, g8, 32] = 1.0
        repadp[2 * g8 + 1, g8, 96] = 1.0
    bd1 = np.zeros((128, 16), np.float32)
    bd2 = np.zeros((128, 16), np.float32)
    for hh in range(16):
        bd1[R1 * hh:R1 * hh + R1, hh] = 1.0
        bd2[R2 * hh:R2 * hh + R2, hh] = 1.0
    return dict(ident=ident, identb=ident.copy(), mask2=mask2, maskqm=maskqm,
                rep1a=rep1a, rep2=rep2, repadp=repadp, bd1=bd1, bd2=bd2)


def kernel(L0_attn_x, L0_attn_Wq1, L0_attn_Wk1, L0_attn_Wq2, L0_attn_Wk2,
           L0_attn_Wv, L0_attn_Wo, L0_attn_bo, L0_ffn_w_gate, L0_ffn_w_up,
           L0_ffn_w_down):
    x = np.ascontiguousarray(np.asarray(L0_attn_x, dtype=np.float32))
    B = x.shape[0]
    shared = dict(
        wq1t=np.ascontiguousarray(np.asarray(L0_attn_Wq1, np.float32).T),
        wk1t=np.ascontiguousarray(np.asarray(L0_attn_Wk1, np.float32).T),
        wq2t=np.ascontiguousarray(np.asarray(L0_attn_Wq2, np.float32).T),
        wk2t=np.ascontiguousarray(np.asarray(L0_attn_Wk2, np.float32).T),
        wvt=np.ascontiguousarray(np.asarray(L0_attn_Wv, np.float32).T),
        wot=np.ascontiguousarray(np.asarray(L0_attn_Wo, np.float32).T),
        bo=np.ascontiguousarray(np.asarray(L0_attn_bo, np.float32).reshape(1, C)),
        wgt=np.ascontiguousarray(np.asarray(L0_ffn_w_gate, np.int32).T),
        wut=np.ascontiguousarray(np.asarray(L0_ffn_w_up, np.int32).T),
        wdt=np.ascontiguousarray(np.asarray(L0_ffn_w_down, np.int32).T),
        **make_consts())
    nc = _get_nc()
    in_maps = [dict(x=np.ascontiguousarray(x[b]), **shared) for b in range(B)]
    res = run_bass_kernel_spmd(nc, in_maps, core_ids=list(range(B)))
    out = np.stack([res.results[b]["y"] for b in range(B)], axis=0)
    return out.astype(np.float32)
